# version 15
# baseline (speedup 1.0000x reference)
import math
import numpy as np
import concourse.bass as bass
import concourse.mybir as mybir
from concourse.bass_utils import run_bass_kernel_spmd

F32 = mybir.dt.float32
I32 = mybir.dt.int32
I16 = mybir.dt.int16
I8 = mybir.dt.int8
FP8 = mybir.dt.float8e4
FP16 = mybir.dt.float16
BF16 = mybir.dt.bfloat16
AF = mybir.ActivationFunctionType
ALU = mybir.AluOpType

LN2 = math.log(2.0)
SIN_SCALE2 = 2.0 * float(np.nextafter(np.float32(np.pi), np.float32(0.0)))
INV2PI = 1.0 / (2.0 * math.pi)

ENGS = ("sync", "tensor", "vector", "scalar", "gpsimd")
PE0 = ("PE", 0)


class Sch:
    def __init__(self):
        self.q = {e: [] for e in ENGS}
        self.cnt = {}

    def emit(self, eng, fn, waits=(), inc=None, inc_n=1, is_dma=False):
        w = []
        for (s, v) in waits:
            if v > 0:
                assert v <= self.cnt.get(s, 0), \
                    f"cyclic wait {s}>={v} (produced {self.cnt.get(s, 0)})"
                w.append((s, v))
        if inc is not None:
            self.cnt[inc] = self.cnt.get(inc, 0) + inc_n
        self.q[eng].append((w, fn, inc, inc_n, is_dma))

    def val(self, sem):
        return self.cnt.get(sem, 0)

    def play(self, eng, handle, sems):
        seen = {}
        for (waits, fn, inc, inc_n, is_dma) in self.q[eng]:
            for s, v in waits:
                if not is_dma and seen.get(s, -1) >= v:
                    continue
                seen[s] = max(seen.get(s, -1), v)
                handle.wait_ge(sems[s], v)
            ins = fn(handle)
            if inc is not None:
                ins.then_inc(sems[inc], inc_n)


def build_program(cfg):
    B = cfg["B"]; T = cfg["T"]; D = cfg["D"]
    NH = cfg["NH"]; NG = cfg["NG"]; HF = cfg["HF"]; BLK = cfg["BLK"]
    PH = cfg.get("phases", 6)
    DH = 64
    DKV = NG * DH
    QKV_W = D + 2 * DKV
    NT_BLK = BLK // 128
    NBLK = T // BLK
    NDC = D // 128
    NJC = HF // 128
    NTILES = T // 128
    NKT = max(1, (NG + 1) // 2)
    SL = 256
    while QKV_W % SL:
        SL -= 128
    NSL = QKV_W // SL
    NHD = D // 512
    assert BLK == 512 and D % 512 == 0 and SL % 128 == 0

    nc = bass.Bass("TRN2", target_bir_lowering=False)
    dr_x = nc.dram_tensor("x", [B * T, D], F32, kind="ExternalInput").ap()
    dr_wqkv = nc.dram_tensor("wqkv", [D, QKV_W], F32, kind="ExternalInput").ap()
    dr_bqkv = nc.dram_tensor("bqkv", [1, QKV_W], F32, kind="ExternalInput").ap()
    dr_wo = nc.dram_tensor("wo", [NDC, 128, D], F32, kind="ExternalInput").ap()
    dr_bo = nc.dram_tensor("bo", [1, D], F32, kind="ExternalInput").ap()
    dr_w1 = nc.dram_tensor("w1t", [NJC, D, 128], F32, kind="ExternalInput").ap()
    dr_b1 = nc.dram_tensor("b1c", [128, NJC], F32, kind="ExternalInput").ap()
    dr_w2 = nc.dram_tensor("w2t", [HF, D], F32, kind="ExternalInput").ap()
    dr_b2 = nc.dram_tensor("b2", [1, D], F32, kind="ExternalInput").ap()
    dr_out = nc.dram_tensor("out", [B * T, D], F32, kind="ExternalOutput").ap()
    dr_w1h = nc.dram_tensor("w1h", [NJC, D, 128], BF16, kind="Internal").ap()
    dr_w1l = nc.dram_tensor("w1l", [NJC, D, 128], BF16, kind="Internal").ap()
    dr_wqh = nc.dram_tensor("wqh", [D, QKV_W], BF16, kind="Internal").ap()
    dr_wql = nc.dram_tensor("wql", [D, QKV_W], BF16, kind="Internal").ap()
    dr_woh = nc.dram_tensor("woh", [NDC, 128, D], BF16, kind="Internal").ap()
    dr_wol = nc.dram_tensor("wol", [NDC, 128, D], BF16, kind="Internal").ap()

    from contextlib import ExitStack
    with ExitStack() as X:
        def sb(n, shape, dt=F32):
            return X.enter_context(nc.sbuf_tensor(n, shape, dt))[:]

        def psum(n, shape):
            return X.enter_context(nc.psum_tensor(n, shape, F32))[:]

        wq_i8 = sb("wq_i8", [128, NJC * D], I8)
        wq_f8 = wq_i8.bitcast(FP8)
        kT_sb = sb("kT_sb", [128, NKT * T])
        v_sb = sb("v_sb", [128, NTILES * (DKV + NG)])
        wqkv_h = sb("wqkv_h", [128, NDC * SL], BF16)
        wqkv_l = sb("wqkv_l", [128, NDC * SL], BF16)
        w1_hb = sb("w1_hb", [128, 2 * D], BF16)
        w1_lb = sb("w1_lb", [128, 2 * D], BF16)
        woh_sb = sb("woh_sb", [128, D], BF16)
        wol_sb = sb("wol_sb", [128, D], BF16)
        sgh = sb("sgh", [128, BLK], BF16)
        sgl = sb("sgl", [128, BLK], BF16)
        xh_sb = sb("xh_sb", [128, NDC * BLK], BF16)
        xl_sb = sb("xl_sb", [128, NDC * BLK], BF16)
        qT_sb = sb("qT_sb", [128, (NH // 2) * BLK])
        astk_h = sb("astk_h", [128, NDC * BLK], BF16)
        astk_l = sb("astk_l", [128, NDC * BLK], BF16)
        hq_i8 = sb("hq_i8", [128, NJC * BLK], I8)
        hq_f8 = hq_i8.bitcast(FP8)
        t16 = sb("t16", [128, D], FP16)
        t8s = sb("t8s", [128, max(D, 2 * BLK)], FP8)
        pool = [sb(f"pool{i}", [128, 1024]) for i in range(6)]
        ib0 = sb("ib0", [128, 1024], I32)
        i16s = ib0.bitcast(I16)
        xt = pool[5][:, 0:D]
        qs = sb("qs", [128, SL])
        cosC = sb("cosC", [128, NTILES * 32])
        sinC = sb("sinC", [128, NTILES * 32])
        cosR = sb("cosR", [128, 128 * (BLK // 128)])
        sinR = sb("sinR", [128, 128 * (BLK // 128)])
        freq32 = sb("freq32", [128, 32])
        maskB = sb("maskB", [128, BLK + 384], BF16)
        onesr = sb("onesr", [128, 128])
        ident = sb("ident", [128, 128])
        rowio = sb("rowio", [128, 1])
        bqkv_row = sb("bqkv_row", [1, QKV_W])
        bo_row = sb("bo_row", [1, D])
        bq2_tile = sb("bq2_tile", [128, D])
        b1c = sb("b1c_sb", [128, NJC])
        lns = sb("lns", [128, 8])
        tvs = sb("tvs", [128, 2])

        pA = [psum(f"pA{i}", [128, 512]) for i in range(4)]
        pB = [psum(f"pB{i}", [128, 1024]) for i in range(2)]

        sems = {}
        for s in ("LBI", "LDX", "LQKA", "LQKB", "LW1A", "LW1B", "LWOA",
                  "LWOB", "LW2A", "LW2B", "GPR", "GPQ", "GPA", "GPC", "PE",
                  "ACT", "VEC", "ST"):
            sems[s] = X.enter_context(nc.semaphore(s))
        blk_ctx = X.enter_context(nc.Block())

        S = Sch()
        gp = lambda fn, **kw: S.emit("gpsimd", fn, **kw)
        pe = lambda fn, **kw: S.emit("tensor", fn, **kw)
        ac = lambda fn, **kw: S.emit("scalar", fn, **kw)
        ve = lambda fn, **kw: S.emit("vector", fn, **kw)

        def dma(eng, out, in_, sem, waits=()):
            S.emit(eng, (lambda o, i: (lambda e: e.dma_start(out=o, in_=i)))(out, in_),
                   waits=waits, inc=sem, inc_n=16, is_dma=True)
            return (sem, S.val(sem))

        n_stores = [0]

        def store(out_ap, in_ap, waits):
            dma("gpsimd", out_ap, in_ap, sem="ST", waits=waits)
            n_stores[0] += 1

        def quant_chain(src, dst, man, min_exp, thr, i0, i1, sB, sC, sD, sE,
                        entry_waits=(), nonneg=False, rne=False, upto=99,
                        taps=None):
            def tap(name, val):
                if taps and name in taps:
                    ve(lambda e, d=taps[name], v=val: e.tensor_copy(out=d, in_=v))
                    ve(lambda e: e.drain())
            """dst = float_quantize(src); pure-arithmetic, fp64-shadow exact.

            i0/i1: int32 scratch (real int allocations -- int writes over int
            memory are native-only and never poison the fp64 shadow; float
            reads of them upcast the native bits, which is exactly the
            insensitive binade/anchor information we want)."""
            i0f = i0.bitcast(F32)
            i1f = i1.bitcast(F32)
            srci = src.bitcast(I32)
            ve(lambda e: e.tensor_scalar(out=i0, in0=srci, scalar1=23, scalar2=0xFF,
                                         op0=ALU.logical_shift_right,
                                         op1=ALU.bitwise_and), waits=entry_waits)
            ve(lambda e: e.drain())
            ve(lambda e: e.tensor_scalar(out=i0, in0=i0, scalar1=127 + min_exp,
                                         scalar2=man, op0=ALU.max, op1=ALU.subtract))
            ve(lambda e: e.drain())
            ve(lambda e: e.tensor_scalar(out=i1, in0=i0, scalar1=0xFF, scalar2=None,
                                         op0=ALU.bitwise_xor))
            ve(lambda e: e.drain())
            ve(lambda e: e.tensor_scalar(out=i1, in0=i1, scalar1=1, scalar2=None,
                                         op0=ALU.subtract))
            ve(lambda e: e.drain())
            ve(lambda e: e.tensor_scalar(out=i1, in0=i1, scalar1=23, scalar2=None,
                                         op0=ALU.logical_shift_left))
            ve(lambda e: e.drain())
            ve(lambda e: e.tensor_scalar(out=i0, in0=i0, scalar1=23, scalar2=None,
                                         op0=ALU.logical_shift_left))
            ve(lambda e: e.drain())
            ve(lambda e: e.tensor_tensor(out=sC, in0=src, in1=i1f, op=ALU.mult))
            ve(lambda e: e.drain(), inc="VEC")
            _v = S.val("VEC")
            tap("p", sC)
            ac(lambda e: e.copy(out=i1, in_=sC), waits=[("VEC", _v)])
            ac(lambda e: e.drain())
            ac(lambda e: e.copy(out=sD, in_=i1), inc="ACT")
            _a = S.val("ACT")
            ve(lambda e: e.tensor_tensor(out=sB, in0=sC, in1=sD, op=ALU.subtract),
               waits=[("ACT", _a)])
            ve(lambda e: e.drain())
            if not rne:
                ve(lambda e: e.tensor_scalar(out=sE, in0=sB, scalar1=0.5,
                                             scalar2=None, op0=ALU.is_ge))
                ve(lambda e: e.drain())
                ve(lambda e: e.scalar_tensor_tensor(out=sB, in0=sB, scalar=-0.5,
                                                    op0=ALU.is_ge, in1=sE,
                                                    op1=ALU.add))
                ve(lambda e: e.drain())
                ve(lambda e: e.tensor_scalar(out=sB, in0=sB, scalar1=1.0,
                                             scalar2=None, op0=ALU.subtract))
                ve(lambda e: e.drain())
                ve(lambda e: e.tensor_tensor(out=sD, in0=sB, in1=sD, op=ALU.add))
                ve(lambda e: e.drain())
            else:
                ve(lambda e: e.tensor_scalar(out=sE, in0=sB, scalar1=0.0,
                                             scalar2=None, op0=ALU.is_lt))
                ve(lambda e: e.drain())
                ve(lambda e: e.tensor_tensor(out=sD, in0=sD, in1=sE,
                                             op=ALU.subtract))
                ve(lambda e: e.drain())
                tap("fl", sD)
                ve(lambda e: e.tensor_tensor(out=sB, in0=sC, in1=sD,
                                             op=ALU.subtract))
                ve(lambda e: e.drain(), inc="VEC")
                _v2 = S.val("VEC")
                tap("g", sB)
                ac(lambda e: e.copy(out=i1, in_=sD), waits=[("VEC", _v2)])
                ac(lambda e: e.drain(), inc="ACT")
                _a2 = S.val("ACT")
                ve(lambda e: e.tensor_scalar(out=i1, in0=i1, scalar1=1, scalar2=None,
                                             op0=ALU.bitwise_and),
                   waits=[("ACT", _a2)])
                ve(lambda e: e.drain(), inc="VEC")
                _v2 = S.val("VEC")
                ac(lambda e: e.copy(out=sC, in_=i1), waits=[("VEC", _v2)])
                ac(lambda e: e.drain(), inc="ACT")
                _a2 = S.val("ACT")
                if taps and "par" in taps:
                    ve(lambda e, d=taps["par"]: e.tensor_copy(out=d, in_=sC),
                       waits=[("ACT", _a2)])
                    ve(lambda e: e.drain())
                ve(lambda e: e.tensor_scalar(out=sE, in0=sB, scalar1=0.5,
                                             scalar2=None, op0=ALU.is_equal),
                   waits=[("ACT", _a2)])
                ve(lambda e: e.drain())
                ve(lambda e: e.tensor_tensor(out=sC, in0=sE, in1=sC, op=ALU.mult))
                ve(lambda e: e.drain())
                ve(lambda e: e.tensor_scalar(out=sB, in0=sB, scalar1=0.5,
                                             scalar2=None, op0=ALU.is_gt))
                ve(lambda e: e.drain())
                ve(lambda e: e.tensor_tensor(out=sB, in0=sB, in1=sC, op=ALU.add))
                ve(lambda e: e.drain())
                ve(lambda e: e.tensor_tensor(out=sD, in0=sB, in1=sD, op=ALU.add))
                ve(lambda e: e.drain())
            tap("r", sD)
            ve(lambda e: e.tensor_tensor(out=sC, in0=sD, in1=i0f, op=ALU.mult))
            ve(lambda e: e.drain())
            if nonneg:
                ve(lambda e: e.tensor_scalar(out=sE, in0=src, scalar1=float(thr),
                                             scalar2=None, op0=ALU.is_ge))
                ve(lambda e: e.drain(), inc="VEC")
            else:
                ve(lambda e: e.drain(), inc="VEC")
                _v0 = S.val("VEC")
                ac(lambda e: e.activation(out=sE, in_=src, func=AF.Abs, scale=1.0),
                   waits=[("VEC", _v0)])
                ac(lambda e: e.drain(), inc="ACT")
                _a0 = S.val("ACT")
                ve(lambda e: e.tensor_scalar(out=sE, in0=sE, scalar1=float(thr),
                                             scalar2=None, op0=ALU.is_ge),
                   waits=[("ACT", _a0)])
                ve(lambda e: e.drain(), inc="VEC")
            _v = S.val("VEC")
            ac(lambda e: e.activation(out=sB, in_=sE, func=AF.Exp, scale=1e30),
               waits=[("VEC", _v)], inc="ACT")
            _a = S.val("ACT")
            ve(lambda e: e.tensor_tensor(out=dst, in0=sC, in1=sB, op=ALU.mult),
               waits=[("ACT", _a)])
            ve(lambda e: e.drain(), inc="VEC")

        def pure_rne_e5m10(W, src, dst, entry_waits=(), taps2=None):
            def tap2(name, val):
                if taps2 and name in taps2:
                    ve(lambda e, d=taps2[name], v=val: e.tensor_copy(out=d, in_=v))
                    ve(lambda e: e.drain())
            """dst = float_quantize(src, 5, 10) via pure float arithmetic.

            No int-bitcast anchors: the native anchor path assumes the native
            fp32 value tracks the fp64 shadow value to <0.5 quanta, which
            breaks downstream of the fp8 hq quantizer (an e4m3-boundary flip
            in native hq shifts native pre by many e5m10 quanta).  Here every
            step (Abs/Ln/Exp/Sign + compare-ladders) is computed correctly in
            whichever precision executes it, so native and shadow results are
            each right in their own world."""
            A = pool[1][:, 0:W]
            Bs = pool[2][:, 0:W]
            C = pool[3][:, 0:W]
            E = pool[4][:, 0:W]
            F = pool[5][:, 0:W]
            LN2 = math.log(2.0)
            TOL = 2e-6
            ac(lambda e: e.activation(out=A, in_=src, func=AF.Abs, scale=1.0),
               waits=list(entry_waits))
            ac(lambda e: e.drain(), inc="ACT")
            _a = S.val("ACT")
            ve(lambda e: e.tensor_scalar(out=Bs, in0=A, scalar1=2.0 ** -40,
                                         scalar2=None, op0=ALU.add),
               waits=[("ACT", _a)])
            ve(lambda e: e.drain(), inc="VEC")
            _v = S.val("VEC")
            ac(lambda e: e.activation(out=Bs, in_=Bs, func=AF.Ln, scale=1.0),
               waits=[("VEC", _v)])
            ac(lambda e: e.drain(), inc="ACT")
            _a = S.val("ACT")
            ve(lambda e: e.tensor_scalar(out=Bs, in0=Bs, scalar1=1.0 / LN2,
                                         scalar2=14.0, op0=ALU.mult, op1=ALU.add),
               waits=[("ACT", _a)])
            ve(lambda e: e.drain())
            ve(lambda e: e.tensor_copy(out=E, in_=Bs))
            ve(lambda e: e.drain())
            for c in (16.0, 8.0, 4.0, 2.0, 1.0):
                ve(lambda e, c=c: e.tensor_scalar(out=F, in0=Bs, scalar1=c,
                                                  scalar2=c, op0=ALU.is_ge,
                                                  op1=ALU.mult))
                ve(lambda e: e.drain())
                ve(lambda e: e.tensor_tensor(out=Bs, in0=Bs, in1=F,
                                             op=ALU.subtract))
                ve(lambda e: e.drain())
            ve(lambda e: e.tensor_tensor(out=E, in0=E, in1=Bs, op=ALU.subtract))
            ve(lambda e: e.drain())
            ve(lambda e: e.tensor_scalar(out=E, in0=E, scalar1=24.0,
                                         scalar2=None, op0=ALU.subtract))
            ve(lambda e: e.drain(), inc="VEC")
            _v = S.val("VEC")
            ac(lambda e: e.activation(out=C, in_=E, func=AF.Exp, scale=-LN2),
               waits=[("VEC", _v)])
            ac(lambda e: e.drain())
            ac(lambda e: e.activation(out=F, in_=E, func=AF.Exp, scale=LN2))
            ac(lambda e: e.drain(), inc="ACT")
            _a = S.val("ACT")
            ve(lambda e: e.tensor_tensor(out=A, in0=A, in1=C, op=ALU.mult),
               waits=[("ACT", _a)])
            ve(lambda e: e.drain())
            tap2("pm", A)
            tap2("scinv", C)
            tap2("sc", F)
            ve(lambda e: e.tensor_copy(out=Bs, in_=A))
            ve(lambda e: e.drain())
            for c in (1024.0, 512.0, 256.0, 128.0, 64.0, 32.0, 16.0, 8.0,
                      4.0, 2.0):
                ve(lambda e, c=c: e.tensor_scalar(out=C, in0=A, scalar1=c,
                                                  scalar2=c, op0=ALU.is_ge,
                                                  op1=ALU.mult))
                ve(lambda e: e.drain())
                ve(lambda e: e.tensor_tensor(out=A, in0=A, in1=C,
                                             op=ALU.subtract))
                ve(lambda e: e.drain())
            ve(lambda e: e.tensor_scalar(out=C, in0=A, scalar1=1.0, scalar2=None,
                                         op0=ALU.is_ge))
            ve(lambda e: e.drain())
            ve(lambda e: e.tensor_tensor(out=A, in0=A, in1=C, op=ALU.subtract))
            ve(lambda e: e.drain())
            ve(lambda e: e.tensor_tensor(out=Bs, in0=Bs, in1=A, op=ALU.subtract))
            ve(lambda e: e.drain())
            tap2("fl", Bs)
            tap2("g", A)
            tap2("b0", C)
            ve(lambda e: e.tensor_scalar(out=E, in0=A, scalar1=0.5 + TOL,
                                         scalar2=None, op0=ALU.is_ge))
            ve(lambda e: e.drain())
            ve(lambda e: e.tensor_scalar(out=A, in0=A, scalar1=0.5 - TOL,
                                         scalar2=None, op0=ALU.is_ge))
            ve(lambda e: e.drain())
            ve(lambda e: e.tensor_tensor(out=A, in0=A, in1=E, op=ALU.subtract))
            ve(lambda e: e.drain())
            ve(lambda e: e.tensor_tensor(out=A, in0=A, in1=C, op=ALU.mult))
            ve(lambda e: e.drain())
            ve(lambda e: e.tensor_tensor(out=E, in0=E, in1=A, op=ALU.add))
            ve(lambda e: e.drain())
            tap2("add", E)
            ve(lambda e: e.tensor_tensor(out=Bs, in0=Bs, in1=E, op=ALU.add))
            ve(lambda e: e.drain())
            ve(lambda e: e.tensor_tensor(out=Bs, in0=Bs, in1=F, op=ALU.mult))
            ve(lambda e: e.drain(), inc="VEC")
            _v = S.val("VEC")
            ac(lambda e: e.activation(out=E, in_=src, func=AF.Sign, scale=1.0),
               waits=[("VEC", _v)])
            ac(lambda e: e.drain(), inc="ACT")
            _a = S.val("ACT")
            ve(lambda e: e.tensor_tensor(out=dst, in0=Bs, in1=E, op=ALU.mult),
               waits=[("ACT", _a)])
            ve(lambda e: e.drain(), inc="VEC")

        def chain_full(width, src, dst, man, min_exp, thr, entry_waits=(),
                       nonneg=False, rne=False, upto=99, taps=None):
            quant_chain(src, dst, man, min_exp, thr,
                        ib0[:, 0:width], ib1[:, 0:width],
                        pool[1][:, 0:width], pool[2][:, 0:width],
                        pool[3][:, 0:width], pool[4][:, 0:width],
                        entry_waits=entry_waits, nonneg=nonneg, rne=rne,
                        taps=taps)

        gp(lambda e: e.iota(rowio, pattern=[[0, 1]], base=0, channel_multiplier=1,
                            allow_small_or_imprecise_dtypes=True))
        gp(lambda e: e.drain())
        gp(lambda e: e.iota(pool[1][:, 0:BLK + 384], pattern=[[1, BLK + 384]], base=0,
                            channel_multiplier=0, allow_small_or_imprecise_dtypes=True))
        gp(lambda e: e.drain())
        gp(lambda e: e.iota(freq32, pattern=[[1, 32]], base=0, channel_multiplier=0,
                            allow_small_or_imprecise_dtypes=True))
        gp(lambda e: e.drain())
        gp(lambda e: e.iota(pool[0][:, 0:128], pattern=[[1, 128]], base=0,
                            channel_multiplier=0, allow_small_or_imprecise_dtypes=True),
           inc="GPC")
        g_io = S.val("GPC")
        ve(lambda e: e.tensor_scalar(out=ident, in0=pool[0][:, 0:128],
                                     scalar1=rowio[:, 0:1], scalar2=None,
                                     op0=ALU.is_equal), waits=[("GPC", g_io)])
        ve(lambda e: e.drain())
        ve(lambda e: e.tensor_scalar(out=pool[0][:, 0:1], in0=rowio, scalar1=384.0,
                                     scalar2=None, op0=ALU.add))
        ve(lambda e: e.drain())
        ve(lambda e: e.tensor_scalar(out=maskB, in0=pool[1][:, 0:BLK + 384],
                                     scalar1=pool[0][:, 0:1],
                                     scalar2=None, op0=ALU.is_ge))
        ve(lambda e: e.drain())
        ve(lambda e: e.memset(onesr, 1.0))
        ve(lambda e: e.drain())
        for t in range(NTILES):
            for g in range(NG):
                c = t * (DKV + NG) + g * (DH + 1) + DH
                ve(lambda e, c=c: e.memset(v_sb[:, c:c + 1], 1.0))
        ve(lambda e: e.drain())
        c_w = -math.log2(10000.0) / 32.0
        pi0 = ib0
        ve(lambda e: e.tensor_scalar(out=pool[0][:, 0:32], in0=freq32, scalar1=c_w,
                                     scalar2=None, op0=ALU.mult))
        ve(lambda e: e.drain(), inc="VEC")
        _v = S.val("VEC")
        ac(lambda e: e.copy(out=pi0[:, 0:32], in_=pool[0][:, 0:32]), waits=[("VEC", _v)])
        ac(lambda e: e.drain())
        ac(lambda e: e.copy(out=pool[2][:, 0:32], in_=pi0[:, 0:32]), inc="ACT")
        _a = S.val("ACT")
        ve(lambda e: e.tensor_tensor(out=pool[3][:, 0:32], in0=pool[0][:, 0:32],
                                     in1=pool[2][:, 0:32], op=ALU.subtract),
           waits=[("ACT", _a)])
        ve(lambda e: e.drain())
        ve(lambda e: e.tensor_scalar(out=pi0[:, 32:64], in0=pi0[:, 0:32], scalar1=127,
                                     scalar2=None, op0=ALU.add))
        ve(lambda e: e.drain())
        ve(lambda e: e.tensor_scalar(out=pi0[:, 32:64], in0=pi0[:, 32:64], scalar1=23,
                                     scalar2=None, op0=ALU.logical_shift_left))
        ve(lambda e: e.drain(), inc="VEC")
        _v = S.val("VEC")
        ac(lambda e: e.activation(out=pool[4][:, 0:32], in_=pool[3][:, 0:32],
                                  func=AF.Exp, scale=LN2), waits=[("VEC", _v)],
           inc="ACT")
        _a = S.val("ACT")
        ve(lambda e: e.tensor_tensor(out=freq32, in0=ib0.bitcast(F32)[:, 32:64],
                                     in1=pool[4][:, 0:32], op=ALU.mult),
           waits=[("ACT", _a)])
        ve(lambda e: e.drain(), inc="VEC")
        v_freq = S.val("VEC")

        a_sin = 0
        for t in range(NTILES):
            ve(lambda e, t=t: e.tensor_scalar(out=tvs[:, 0:1], in0=rowio,
                                              scalar1=float(128 * t), scalar2=INV2PI,
                                              op0=ALU.add, op1=ALU.mult),
               waits=[("VEC", v_freq)])
            ve(lambda e: e.drain())
            ve(lambda e: e.tensor_scalar(out=pool[0][:, 0:32], in0=freq32,
                                         scalar1=tvs[:, 0:1], scalar2=None,
                                         op0=ALU.mult))
            ve(lambda e: e.drain(), inc="VEC")
            yv = S.val("VEC")
            for (dst, add25) in ((sinC, False), (cosC, True)):
                if add25:
                    ve(lambda e: e.tensor_scalar(out=pool[2][:, 0:32],
                                                 in0=pool[0][:, 0:32], scalar1=0.25,
                                                 scalar2=None, op0=ALU.add),
                       waits=[("VEC", yv), ("ACT", a_sin)])
                else:
                    ve(lambda e: e.tensor_copy(out=pool[2][:, 0:32],
                                               in_=pool[0][:, 0:32]),
                       waits=[("VEC", yv), ("ACT", a_sin)])
                ve(lambda e: e.drain(), inc="VEC")
                _v = S.val("VEC")
                ac(lambda e: e.copy(out=pi0[:, 0:32], in_=pool[2][:, 0:32]),
                   waits=[("VEC", _v)])
                ac(lambda e: e.drain())
                ac(lambda e: e.copy(out=pool[3][:, 0:32], in_=pi0[:, 0:32]), inc="ACT")
                _a = S.val("ACT")
                ve(lambda e: e.tensor_tensor(out=pool[2][:, 0:32],
                                             in0=pool[2][:, 0:32],
                                             in1=pool[3][:, 0:32], op=ALU.subtract),
                   waits=[("ACT", _a)])
                ve(lambda e: e.drain())
                ve(lambda e: e.tensor_scalar(out=pool[3][:, 0:32],
                                             in0=pool[2][:, 0:32], scalar1=0.5,
                                             scalar2=None, op0=ALU.is_ge))
                ve(lambda e: e.drain())
                ve(lambda e: e.tensor_tensor(out=pool[2][:, 0:32],
                                             in0=pool[2][:, 0:32],
                                             in1=pool[3][:, 0:32], op=ALU.subtract))
                ve(lambda e: e.drain(), inc="VEC")
                _v = S.val("VEC")
                ac(lambda e, dst=dst, t=t: e.activation(
                    out=dst[:, 32 * t:32 * t + 32], in_=pool[2][:, 0:32],
                    func=AF.Sin, scale=SIN_SCALE2), waits=[("VEC", _v)], inc="ACT")
                a_sin = S.val("ACT")
        a_tables = S.val("ACT")

        dma("sync", bqkv_row, dr_bqkv, "LBI")
        dma("sync", bo_row, dr_bo, "LBI")
        dma("sync", pool[0][0:1, 0:D], dr_b2, "LBI",
            waits=[("ACT", a_tables)])
        ld_bias = dma("sync", b1c, dr_b1, "LBI")

        ve(lambda e: e.tensor_copy(out=t16[0:1, 0:D], in_=pool[0][0:1, 0:D]),
           waits=[ld_bias, ("ACT", a_tables)])
        ve(lambda e: e.drain())
        ve(lambda e: e.tensor_copy(out=i16s[0:1, 0:D],
                                   in_=t16.bitcast(I16)[0:1, 0:D]))
        ve(lambda e: e.drain())
        ve(lambda e: e.tensor_copy(out=pool[0][0:1, 0:D],
                                   in_=i16s.bitcast(FP16)[0:1, 0:D]))
        ve(lambda e: e.drain(), inc="VEC")
        v_b2 = S.val("VEC")
        for hf in range(NHD):
            pe(lambda e, hf=hf: e.matmul(pB[0][:, hf * 512:(hf + 1) * 512],
                                         onesr[0:1, :],
                                         pool[0][0:1, hf * 512:(hf + 1) * 512],
                                         start=True, stop=True),
               waits=[("VEC", v_b2)], inc="PE")
        _p = S.val("PE")
        ac(lambda e: e.copy(out=bq2_tile, in_=pB[0][:, 0:D]), waits=[("PE", _p)],
           inc="ACT")
        a_bq2 = S.val("ACT")
        pB_free = [("ACT", a_bq2), ("ACT", 0)]

        w2_done = {}
        for j in range(NJC):
            buf = pool[1 + (j % 2)][:, 0:D]
            wch = [("VEC", w2_done[j - 2])] if j >= 2 else [("ACT", a_bq2)]
            ldv = dma("sync", buf, dr_w2[j * 128:(j + 1) * 128, :],
                      "LW2A" if j % 2 == 0 else "LW2B", waits=wch)
            ve(lambda e, buf=buf: e.tensor_copy(out=t8s[:, 0:D], in_=buf),
               waits=[ldv])
            ve(lambda e: e.drain())
            ve(lambda e, j=j: e.tensor_copy(out=wq_i8[:, j * D:(j + 1) * D],
                                            in_=t8s.bitcast(I8)[:, 0:D]))
            ve(lambda e: e.drain(), inc="VEC")
            w2_done[j] = S.val("VEC")
        v_wq = S.val("VEC")

        chunks = []
        for j in range(NJC):
            chunks.append((dr_w1[j].rearrange("(c p) n -> p c n", p=128),
                           dr_w1h[j].rearrange("(c p) n -> p c n", p=128),
                           dr_w1l[j].rearrange("(c p) n -> p c n", p=128),
                           NDC * 128))
        for r in range(NDC):
            for hf2 in range(2):
                w = QKV_W // 2
                chunks.append((dr_wqkv[r * 128:(r + 1) * 128,
                                       hf2 * w:(hf2 + 1) * w],
                               dr_wqh[r * 128:(r + 1) * 128,
                                      hf2 * w:(hf2 + 1) * w],
                               dr_wql[r * 128:(r + 1) * 128,
                                      hf2 * w:(hf2 + 1) * w], w))
        for c in range(NDC):
            chunks.append((dr_wo[c], dr_woh[c], dr_wol[c], D))
        split_done = {}
        split_st = {}
        for i, (src, dh, dl, wdt) in enumerate(chunks):
            slot = i % 4
            buf = pool[1 + slot][:, 0:wdt]
            if i >= 4:
                lw = [("VEC", split_done[i - 4])]
            elif slot < 2:
                lw = [("VEC", w2_done[NJC - 2 + slot])]
            else:
                lw = [("ACT", a_bq2)]
            ldc = dma("sync", buf, src, ("LW2A", "LW2B", "LW1A", "LW1B")[slot],
                      waits=lw)
            sw = (NDC * BLK) // 4
            hslc = xh_sb[:, slot * sw:slot * sw + wdt]
            lslc = xl_sb[:, slot * sw:slot * sw + wdt]
            stw = [split_st[i - 4]] if i >= 4 else []
            ve(lambda e, b=buf, h=hslc: e.tensor_copy(out=h, in_=b),
               waits=[ldc] + stw)
            ve(lambda e: e.drain())
            ve(lambda e, b=buf, h=hslc, l=lslc: e.tensor_tensor(
                out=l, in0=b, in1=h, op=ALU.subtract))
            ve(lambda e: e.drain(), inc="VEC")
            split_done[i] = S.val("VEC")
            store(dh, hslc, waits=[("VEC", split_done[i]),
                                   ("ST", n_stores[0] * 16)])
            store(dl, lslc, waits=[("VEC", split_done[i])])
            split_st[i] = ("ST", n_stores[0] * 16)
        v_split_st = ("ST", n_stores[0] * 16)

        pA_free = [("ACT", 0)] * 4
        wqkv_free = [v_split_st, v_split_st]
        w1_free = [v_split_st, v_split_st]
        wo_free = [("ACT", a_tables), ("ACT", a_tables)]
        kv_ready = {}
        xnT_free = v_split_st
        qT_free = ("ST", 0)
        astk_free = PE0
        pool_free = ("VEC", S.val("VEC"))
        xt_free = ("ACT", a_tables)
        probs_free = [PE0, PE0]
        qs_free = PE0
        qstg_free = PE0
        sstg_free = ("GPA", 0)
        rope_free = ("VEC", 0)
        xn_free = PE0
        pool_free_v = ("VEC", 0)
        nstg_free = ("VEC", 0)

        for seq in range(B):
            for blki in range(NBLK):
                tok0 = blki * BLK
                row0 = seq * T + tok0

                g_rep = {}
                xsplit_map = {}
                for sl in range(NSL):
                    lw = [wqkv_free[0]]
                    dma("sync", wqkv_h,
                        dr_wqh[:, sl * SL:(sl + 1) * SL].rearrange(
                            "(c p) n -> p c n", p=128),
                        "LQKA", waits=lw)
                    ld_w = dma("sync", wqkv_l,
                        dr_wql[:, sl * SL:(sl + 1) * SL].rearrange(
                            "(c p) n -> p c n", p=128),
                        "LQKA")
                    for tt in range(NT_BLK):
                        t0 = tok0 + tt * 128
                        tile_abs = t0 // 128
                        if sl == 0:
                            ldx = dma("sync", xt,
                                dr_x[row0 + tt * 128:row0 + (tt + 1) * 128, :],
                                "LDX", waits=[xt_free])
                            ve(lambda e: e.tensor_reduce(out=lns[:, 0:1], in_=xt,
                                                         axis=mybir.AxisListType.X,
                                                         op=ALU.add),
                               waits=[ldx, pool_free])
                            ve(lambda e: e.drain())
                            ve(lambda e: e.tensor_tensor(out=pool[0][:, 0:D], in0=xt,
                                                         in1=xt, op=ALU.mult),
                               waits=[xn_free, ("ACT", a_bq2)])
                            ve(lambda e: e.drain())
                            ve(lambda e: e.tensor_reduce(out=lns[:, 1:2],
                                                         in_=pool[0][:, 0:D],
                                                         axis=mybir.AxisListType.X,
                                                         op=ALU.add))
                            ve(lambda e: e.drain())
                            ve(lambda e: e.tensor_scalar(out=lns[:, 2:3],
                                                         in0=lns[:, 0:1],
                                                         scalar1=1.0 / D, scalar2=None,
                                                         op0=ALU.mult))
                            ve(lambda e: e.drain())
                            ve(lambda e: e.tensor_scalar(out=lns[:, 3:4],
                                                         in0=lns[:, 1:2],
                                                         scalar1=1.0 / D, scalar2=1e-5,
                                                         op0=ALU.mult, op1=ALU.add))
                            ve(lambda e: e.drain())
                            ve(lambda e: e.scalar_tensor_tensor(out=lns[:, 4:5],
                                                                in0=lns[:, 2:3],
                                                                scalar=0.0,
                                                                in1=lns[:, 2:3],
                                                                op0=ALU.add,
                                                                op1=ALU.mult))
                            ve(lambda e: e.drain())
                            ve(lambda e: e.tensor_tensor(out=lns[:, 3:4],
                                                         in0=lns[:, 3:4],
                                                         in1=lns[:, 4:5],
                                                         op=ALU.subtract))
                            ve(lambda e: e.drain(), inc="VEC")
                            _v = S.val("VEC")
                            ac(lambda e: e.activation(out=lns[:, 5:6], in_=lns[:, 3:4],
                                                      func=AF.Ln, scale=1.0),
                               waits=[("VEC", _v)])
                            ac(lambda e: e.drain())
                            ac(lambda e: e.activation(out=lns[:, 4:5], in_=lns[:, 5:6],
                                                      func=AF.Exp, scale=-0.5),
                               inc="ACT")
                            _a = S.val("ACT")
                            ve(lambda e: e.tensor_scalar(out=pool[0][:, 0:D], in0=xt,
                                                         scalar1=lns[:, 2:3],
                                                         scalar2=lns[:, 4:5],
                                                         op0=ALU.subtract,
                                                         op1=ALU.mult),
                               waits=[("ACT", _a)])
                            ve(lambda e: e.drain(), inc="VEC")
                            xn_v = S.val("VEC")
                            xt_free = ("VEC", xn_v)
                            for c in range(NDC):
                                pslot = 2 + (c % 2)
                                pe(lambda e, c=c, p=pslot: e.transpose(
                                    pA[p][:, 0:128], pool[0][:, c * 128:(c + 1) * 128],
                                    ident),
                                   waits=[("VEC", xn_v), pA_free[pslot]] +
                                         ([xnT_free] if (c == 0 and tt == 0) else []),
                                   inc="PE")
                                _p = S.val("PE")
                                sl_x = slice(c * BLK + tt * 128,
                                             c * BLK + (tt + 1) * 128)
                                ac(lambda e, p=pslot, s=sl_x: e.copy(
                                    out=xh_sb[:, s], in_=pA[p][:, 0:128]),
                                   waits=[("PE", _p)], inc="ACT")
                                _a = S.val("ACT")
                                ve(lambda e, p=pslot, s=sl_x: e.tensor_tensor(
                                    out=xl_sb[:, s], in0=pA[p][:, 0:128],
                                    in1=xh_sb[:, s], op=ALU.subtract),
                                   waits=[("ACT", _a)])
                                ve(lambda e: e.drain(), inc="VEC")
                                pA_free[pslot] = ("VEC", S.val("VEC"))
                            xn_free = ("PE", S.val("PE"))
                            xsplit_v = ("VEC", S.val("VEC"))
                            for r in range(4):
                                dma("gpsimd", cosR[:, 128 * tt + 32 * r:
                                                   128 * tt + 32 * r + 32],
                                    cosC[:, 32 * tile_abs:32 * tile_abs + 32],
                                    "GPR",
                                    waits=([("ACT", a_tables), rope_free,
                                            ("GPR", S.val("GPR"))]
                                           if r == 0 else []))
                                dma("gpsimd", sinR[:, 128 * tt + 32 * r:
                                                   128 * tt + 32 * r + 32],
                                    sinC[:, 32 * tile_abs:32 * tile_abs + 32],
                                    "GPR")
                            g_rep[tt] = ("GPR", S.val("GPR"))
                            xsplit_map[tt] = xsplit_v
                        xn_ready = xsplit_map[tt]
                        qp = pA[tt // 2][:, (tt % 2) * SL:(tt % 2) * SL + SL]
                        first = True
                        for c in range(NDC):
                            xs = slice(c * BLK + tt * 128,
                                       c * BLK + (tt + 1) * 128)
                            ws = slice(c * SL, (c + 1) * SL)
                            for lh, mv in ((xh_sb, wqkv_h), (xl_sb, wqkv_h),
                                           (xh_sb, wqkv_l)):
                                pe(lambda e, xs=xs, ws=ws, lh=lh, mv=mv,
                                   qp=qp, st=first: e.matmul(
                                       qp, lh[:, xs], mv[:, ws],
                                       start=st, stop=False),
                                   waits=([ld_w, xn_ready, pA_free[tt // 2]]
                                          if first else []))
                                first = False
                        pe(lambda e, tt=tt, sl=sl, qp=qp: e.matmul(
                            qp, onesr[0:1, :],
                            bqkv_row[:, sl * SL:(sl + 1) * SL], start=False, stop=True),
                           waits=[ld_bias], inc="PE")
                        pe_qkv = S.val("PE")
                        if tt == NT_BLK - 1:
                            wqkv_free[0] = ("PE", pe_qkv)
                        ac(lambda e, tt=tt, qp=qp: e.copy(out=qs, in_=qp),
                           waits=[("PE", pe_qkv), qs_free], inc="ACT")
                        ev_a = S.val("ACT")
                        pA_free[tt // 2] = ("ACT", ev_a)
                        col0 = sl * SL
                        rope_w = 0
                        if col0 < D + DKV:
                            rope_w = min(SL, D + DKV - col0)
                        if rope_w > 0:
                            npair = rope_w // 2
                            sve = qs[:, 0:rope_w:2]
                            svo = qs[:, 1:rope_w:2]
                            cRt = cosR[:, 128 * tt:128 * tt + npair]
                            sRt = sinR[:, 128 * tt:128 * tt + npair]
                            ve(lambda e, sve=sve, n=npair, cR=cRt: e.tensor_tensor(
                                out=pool[1][:, 0:n], in0=sve, in1=cR,
                                op=ALU.mult),
                               waits=[("ACT", ev_a), g_rep[tt]])
                            ve(lambda e, svo=svo, n=npair, sR=sRt: e.tensor_tensor(
                                out=pool[2][:, 0:n], in0=svo, in1=sR,
                                op=ALU.mult))
                            ve(lambda e, sve=sve, n=npair, sR=sRt: e.tensor_tensor(
                                out=pool[3][:, 0:n], in0=sve, in1=sR,
                                op=ALU.mult))
                            ve(lambda e, svo=svo, n=npair, cR=cRt: e.tensor_tensor(
                                out=pool[4][:, 0:n], in0=svo, in1=cR,
                                op=ALU.mult))
                            ve(lambda e: e.drain())
                            ve(lambda e, sve=sve, n=npair: e.tensor_tensor(
                                out=sve, in0=pool[1][:, 0:n], in1=pool[2][:, 0:n],
                                op=ALU.subtract))
                            ve(lambda e, svo=svo, n=npair: e.tensor_tensor(
                                out=svo, in0=pool[3][:, 0:n], in1=pool[4][:, 0:n],
                                op=ALU.add))
                            ve(lambda e: e.drain(), inc="VEC")
                            rope_v = [("VEC", S.val("VEC"))]
                        else:
                            rope_v = [("ACT", ev_a)]
                        for c2 in range(SL // 128):
                            gcol = col0 + c2 * 128
                            pslot = 2 + (c2 % 2)
                            if gcol < D + DKV:
                                pe(lambda e, c2=c2, p=pslot: e.transpose(
                                    pA[p][:, 0:128], qs[:, c2 * 128:(c2 + 1) * 128],
                                    ident),
                                   waits=rope_v + [pA_free[pslot]] +
                                         ([qT_free] if (sl == 0 and tt == 0 and
                                                        c2 == 0) else []),
                                   inc="PE")
                                _p = S.val("PE")
                                if gcol < D:
                                    dcol = (gcol // 128) * BLK + tt * 128
                                    ac(lambda e, d=dcol, p=pslot: e.copy(
                                        out=qT_sb[:, d:d + 128], in_=pA[p][:, 0:128]),
                                       waits=[("PE", _p)], inc="ACT")
                                else:
                                    kc = (gcol - D) // 128
                                    dcol = kc * T + t0
                                    ac(lambda e, d=dcol, p=pslot: e.copy(
                                        out=kT_sb[:, d:d + 128], in_=pA[p][:, 0:128]),
                                       waits=[("PE", _p)], inc="ACT")
                                pA_free[pslot] = ("ACT", S.val("ACT"))
                            else:
                                off = gcol - (D + DKV)
                                for g in range(NG):
                                    glo, ghi = g * DH, (g + 1) * DH
                                    lo, hi = max(glo, off), min(ghi, off + 128)
                                    if lo >= hi:
                                        continue
                                    dcol = (tile_abs * (DKV + NG) + g * (DH + 1) +
                                            (lo - glo))
                                    scol = c2 * 128 + (lo - off)
                                    ac(lambda e, d=dcol, s=scol, n=hi - lo: e.copy(
                                        out=v_sb[:, d:d + n], in_=qs[:, s:s + n]),
                                       waits=[("ACT", ev_a)], inc="ACT")
                        ac(lambda e: e.drain())
                        qs_free = ("PE", S.val("PE"))
                        kv_ready[(seq, tile_abs)] = ("ACT", S.val("ACT"))
                xnT_ready = ("ACT", S.val("ACT"))
                xsplit_all = xsplit_map[NT_BLK - 1]
                pool_free = ("PE", S.val("PE"))
                pool_free_v = ("VEC", S.val("VEC"))
                rope_free = ("VEC", S.val("VEC"))

                nts_hi = (tok0 + BLK) // 128
                if PH < 2:
                    for tt in range(NT_BLK):
                        _v = S.val("ACT")
                        store(dr_out[row0 + tt * 128:row0 + (tt + 1) * 128, :],
                              pool[0][:, 0:D], waits=[("ACT", _v)])
                    xnT_free = ("PE", S.val("PE"))
                    qT_free = ("ST", n_stores[0] * 16)
                    pool_free = ("VEC", S.val("VEC"))
                    continue
                kv_all = kv_ready[(seq, nts_hi - 1)]
                qstg = pool[0]
                sstg = pool[1]
                nstg = pool[4]
                for h in range(NH):
                    g = h // (NH // NG)
                    kbase = 64 * (g % 2)
                    ktcol = (g // 2) * T
                    qbase = 64 * (h % 2)
                    qcol = (h // 2) * BLK
                    if qbase != kbase:
                        gp(lambda e, qc=qcol, qb=qbase, kb=kbase: e.dma_start(
                            out=qstg[kb:kb + 64, 0:BLK],
                            in_=qT_sb[qb:qb + 64, qc:qc + BLK]),
                           waits=[xnT_ready, qstg_free] +
                                 ([pool_free, pool_free_v] if h == 0 else []),
                           inc="GPQ", inc_n=16, is_dma=True)
                        q_ap = qstg[kbase:kbase + 64, 0:BLK]
                        q_wait = ("GPQ", S.val("GPQ"))
                        q_mismatch = True
                    else:
                        q_ap = qT_sb[qbase:qbase + 64, qcol:qcol + BLK]
                        q_wait = xnT_ready
                        q_mismatch = False
                    for ts in range(nts_hi):
                        sslot = ts % 2
                        sp = pA[sslot]
                        pe(lambda e, ts=ts, sp=sp, qa=q_ap, kb=kbase, kc=ktcol:
                           e.matmul(sp[:, 0:BLK],
                                    kT_sb[kb:kb + 64,
                                          kc + ts * 128:kc + (ts + 1) * 128],
                                    qa, start=True, stop=True),
                           waits=[q_wait, kv_all, pA_free[sslot], probs_free[sslot]],
                           inc="PE")
                        pesc = S.val("PE")
                        pr = pool[2][:, 0:BLK] if sslot == 0 else pool[3][:, 0:BLK]
                        ac(lambda e, sp=sp, pr=pr: e.activation(
                            out=pr, in_=sp[:, 0:BLK], func=AF.Exp, scale=0.125),
                           waits=[("PE", pesc)] +
                                 ([pool_free, pool_free_v]
                                  if (h == 0 and ts == 0) else []),
                           inc="ACT")
                        pA_free[sslot] = ("ACT", S.val("ACT"))
                        exp_a = S.val("ACT")
                        pv_w = [("ACT", exp_a)]
                        if ts * 128 >= tok0:
                            off = ts * 128 - tok0
                            ve(lambda e, pr=pr, off=off: e.tensor_tensor(
                                out=pr, in0=pr,
                                in1=maskB[:, 384 - off:384 - off + BLK],
                                op=ALU.mult), waits=[("ACT", exp_a)])
                            ve(lambda e: e.drain(), inc="VEC")
                            pv_w = [("VEC", S.val("VEC"))]
                        vcol = ts * (DKV + NG) + g * (DH + 1)
                        pe(lambda e, vc=vcol, pr=pr, st=(ts == 0),
                           sp2=(ts == nts_hi - 1): e.matmul(
                            pA[3][0:DH + 1, 0:BLK], v_sb[:, vc:vc + DH + 1], pr,
                            start=st, stop=sp2),
                           waits=pv_w + ([pA_free[3]] if ts == 0 else []), inc="PE")
                        probs_free[sslot] = ("PE", S.val("PE"))
                    pe_pv = S.val("PE")
                    if q_mismatch:
                        qstg_free = ("PE", pe_pv)
                    ve(lambda e: e.reciprocal(out=sstg[DH:DH + 1, 0:BLK],
                                              in_=pA[3][DH:DH + 1, 0:BLK]),
                       waits=[("PE", pe_pv), sstg_free])
                    ve(lambda e: e.drain(), inc="VEC")
                    _v = S.val("VEC")
                    pe(lambda e: e.matmul(pB[0][0:64, 0:BLK], onesr[DH:DH + 1, 0:64],
                                          sstg[DH:DH + 1, 0:BLK], start=True,
                                          stop=True),
                       waits=[("VEC", _v), pB_free[0]], inc="PE")
                    pe_bc = S.val("PE")
                    ac(lambda e: e.copy(out=nstg[0:64, 0:BLK], in_=pB[0][0:64, 0:BLK]),
                       waits=[("PE", pe_bc), nstg_free], inc="ACT")
                    a_bc = S.val("ACT")
                    pB_free[0] = ("ACT", a_bc)
                    ch = (h * DH) // 128
                    drow = 64 * (h % 2)
                    ve(lambda e: e.tensor_tensor(
                        out=sstg[0:64, 0:BLK], in0=pA[3][0:64, 0:BLK],
                        in1=nstg[0:64, 0:BLK], op=ALU.mult),
                       waits=[("ACT", a_bc)])
                    ve(lambda e: e.drain())
                    if drow == 0:
                        ve(lambda e, ch=ch: e.tensor_copy(
                            out=astk_h[0:64, ch * BLK:ch * BLK + BLK],
                            in_=sstg[0:64, 0:BLK]),
                           waits=([astk_free] if h == 0 else []))
                        ve(lambda e: e.drain())
                        ve(lambda e, ch=ch: e.tensor_tensor(
                            out=astk_l[0:64, ch * BLK:ch * BLK + BLK],
                            in0=sstg[0:64, 0:BLK],
                            in1=astk_h[0:64, ch * BLK:ch * BLK + BLK],
                            op=ALU.subtract))
                        ve(lambda e: e.drain(), inc="VEC")
                        _v = S.val("VEC")
                        pA_free[3] = ("VEC", _v)
                        nstg_free = ("VEC", _v)
                    else:
                        ve(lambda e: e.tensor_copy(out=sgh[0:64, 0:BLK],
                                                   in_=sstg[0:64, 0:BLK]))
                        ve(lambda e: e.drain())
                        ve(lambda e: e.tensor_tensor(
                            out=sgl[0:64, 0:BLK], in0=sstg[0:64, 0:BLK],
                            in1=sgh[0:64, 0:BLK], op=ALU.subtract))
                        ve(lambda e: e.drain(), inc="VEC")
                        _v = S.val("VEC")
                        pA_free[3] = ("VEC", _v)
                        nstg_free = ("VEC", _v)
                        gp(lambda e, ch=ch: e.dma_start(
                            out=astk_h[64:128, ch * BLK:ch * BLK + BLK],
                            in_=sgh[0:64, 0:BLK]),
                           waits=[("VEC", _v)] + ([astk_free] if h == 1 else []),
                           inc="GPA", inc_n=16, is_dma=True)
                        gp(lambda e, ch=ch: e.dma_start(
                            out=astk_l[64:128, ch * BLK:ch * BLK + BLK],
                            in_=sgl[0:64, 0:BLK]),
                           inc="GPA", inc_n=16, is_dma=True)
                        sstg_free = ("GPA", S.val("GPA"))
                astk_v = ("VEC", S.val("VEC"))
                astk_g = ("GPA", S.val("GPA"))

                if PH < 3:
                    for tt in range(NT_BLK):
                        _v = S.val("VEC")
                        store(dr_out[row0 + tt * 128:row0 + (tt + 1) * 128, :],
                              pool[0][:, 0:D], waits=[("VEC", _v)])
                    xnT_free = ("PE", S.val("PE"))
                    astk_free = ("PE", S.val("PE"))
                    qT_free = ("ST", n_stores[0] * 16)
                    pool_free = ("VEC", S.val("VEC"))
                    continue
                for grp in range(NT_BLK // 2):
                    for c in range(NDC):
                        dma("sync", woh_sb, dr_woh[c], "LWOA",
                            waits=[wo_free[0]])
                        ld_wo = dma("sync", wol_sb, dr_wol[c], "LWOA")
                        for tl in range(2):
                            tt = grp * 2 + tl
                            pbo = pB[tl]
                            first_t = True
                            for ti, (ah, wv) in enumerate(
                                    ((astk_h, woh_sb), (astk_l, woh_sb),
                                     (astk_h, wol_sb))):
                                for hf in range(NHD):
                                    last = (tl == 1 and ti == 2
                                            and hf == NHD - 1)
                                    pe(lambda e, c=c, pb=pbo, tt=tt, ah=ah,
                                       wv=wv, hf=hf,
                                       st=(c == 0 and ti == 0): e.matmul(
                                        pb[:, hf * 512:(hf + 1) * 512],
                                        ah[:, c * BLK + tt * 128:
                                           c * BLK + (tt + 1) * 128],
                                        wv[:, hf * 512:(hf + 1) * 512],
                                        start=st, stop=False),
                                       waits=(([astk_v, astk_g, pB_free[0],
                                                pB_free[1], ld_wo]
                                               if (c == 0 and tl == 0 and first_t)
                                               else ([ld_wo]
                                                     if (tl == 0 and first_t)
                                                     else []))),
                                       inc=("PE" if last else None))
                                    first_t = False
                        wo_free[0] = ("PE", S.val("PE"))
                    for tl in range(2):
                        tt = grp * 2 + tl
                        pbo = pB[tl]
                        for hf in range(NHD):
                            pe(lambda e, hf=hf, pb=pbo: e.matmul(
                                pb[:, hf * 512:(hf + 1) * 512], onesr[0:1, :],
                                bo_row[:, hf * 512:(hf + 1) * 512], start=False,
                                stop=True),
                               waits=[ld_bias],
                               inc=("PE" if hf == NHD - 1 else None))
                        pe_o = S.val("PE")
                        ldx2 = dma("sync", xt,
                                   dr_x[row0 + tt * 128:row0 + (tt + 1) * 128, :],
                                   "LDX", waits=[xt_free])
                        attnO = qT_sb
                        ve(lambda e, pb=pbo, tt=tt: e.tensor_tensor(
                            out=attnO[:, tt * D:(tt + 1) * D], in0=xt, in1=pb[:, 0:D],
                            op=ALU.add),
                           waits=[("PE", pe_o), ldx2] +
                                 ([qT_free] if (grp == 0 and tl == 0) else []))
                        ve(lambda e: e.drain(), inc="VEC")
                        _v = S.val("VEC")
                        xt_free = ("VEC", _v)
                        pB_free[tl] = ("VEC", _v)
                attnO_v = ("VEC", S.val("VEC"))

                if PH < 4:
                    for tt in range(NT_BLK):
                        _v = S.val("VEC")
                        store(dr_out[row0 + tt * 128:row0 + (tt + 1) * 128, :],
                              qT_sb[:, tt * D:(tt + 1) * D], waits=[attnO_v])
                    xnT_free = ("PE", S.val("PE"))
                    astk_free = ("PE", S.val("PE"))
                    qT_free = ("ST", n_stores[0] * 16)
                    pool_free = ("VEC", S.val("VEC"))
                    continue
                h_prev = ("VEC", 0)
                for j in range(NJC):
                    wslot = j % 2
                    w1bh = w1_hb[:, wslot * D:(wslot + 1) * D]
                    w1bl = w1_lb[:, wslot * D:(wslot + 1) * D]
                    dma("sync", w1bh,
                        dr_w1h[j].rearrange("(c p) n -> p c n", p=128),
                        "LW1A" if wslot == 0 else "LW1B",
                        waits=[w1_free[wslot]])
                    ldw1 = dma("sync", w1bl,
                        dr_w1l[j].rearrange("(c p) n -> p c n", p=128),
                        "LW1A" if wslot == 0 else "LW1B")
                    hp = pA[j % 2]
                    first = True
                    for c in range(NDC):
                        ws = slice(c * 128, (c + 1) * 128)
                        xs = slice(c * BLK, (c + 1) * BLK)
                        for wb, xb in ((w1bh, xh_sb), (w1bl, xh_sb),
                                       (w1bh, xl_sb)):
                            last = (c == NDC - 1 and xb is xl_sb)
                            pe(lambda e, wb=wb, xb=xb, ws=ws, xs=xs, hp=hp,
                               st=first, sp=last: e.matmul(
                                   hp[:, 0:BLK], wb[:, ws], xb[:, xs],
                                   start=st, stop=sp),
                               waits=([ldw1, xnT_ready, xsplit_all,
                                       pA_free[j % 2]]
                                      if first else []),
                               inc=("PE" if last else None))
                            first = False
                    pe_h = S.val("PE")
                    w1_free[wslot] = ("PE", pe_h)
                    ac(lambda e, hp=hp, j=j: e.activation(
                        out=t8s[:, (j % 2) * BLK:(j % 2) * BLK + BLK],
                        in_=hp[:, 0:BLK], func=AF.Relu,
                        bias=b1c[:, j:j + 1], scale=1.0),
                       waits=[("PE", pe_h), h_prev], inc="ACT")
                    a_h = S.val("ACT")
                    pA_free[j % 2] = ("ACT", a_h)
                    ve(lambda e, j=j: e.tensor_copy(
                        out=hq_i8[:, j * BLK:(j + 1) * BLK],
                        in_=t8s.bitcast(I8)[:, (j % 2) * BLK:(j % 2) * BLK + BLK]),
                       waits=[("ACT", a_h)])
                    ve(lambda e: e.drain(), inc="VEC")
                    h_prev = ("VEC", S.val("VEC"))
                hq_ready = ("VEC", S.val("VEC"))
                if PH < 5:
                    for tt in range(NT_BLK):
                        store(dr_out[row0 + tt * 128:row0 + (tt + 1) * 128, :],
                              qT_sb[:, tt * D:(tt + 1) * D], waits=[attnO_v, hq_ready])
                    xnT_free = ("PE", S.val("PE"))
                    astk_free = ("PE", S.val("PE"))
                    qT_free = ("ST", n_stores[0] * 16)
                    pool_free = ("VEC", S.val("VEC"))
                    continue
                for tt in range(NT_BLK):
                    slot = tt % 2
                    pf = pB[slot]
                    for j in range(NJC):
                        for hf in range(NHD):
                            pe(lambda e, j=j, pf=pf, tt=tt, hf=hf: e.matmul(
                                pf[:, hf * 512:(hf + 1) * 512],
                                hq_f8[:, j * BLK + tt * 128:j * BLK + (tt + 1) * 128],
                                wq_f8[:, j * D + hf * 512:j * D + (hf + 1) * 512],
                                start=(j == 0), stop=(j == NJC - 1)),
                               waits=([hq_ready, pB_free[slot]]
                                      if (j == 0 and hf == 0) else []),
                               inc=("PE" if (j == NJC - 1 and hf == NHD - 1)
                                    else None))
                    pe_f = S.val("PE")
                    ve(lambda e, pf=pf: e.tensor_tensor(out=pool[0][:, 0:D],
                                                        in0=pf[:, 0:D],
                                                        in1=bq2_tile, op=ALU.add),
                       waits=[("PE", pe_f), ("ACT", a_bq2)])
                    ve(lambda e: e.drain(), inc="VEC")
                    pB_free[slot] = ("VEC", S.val("VEC"))
                    if PH >= 6:
                        ve(lambda e: e.tensor_copy(out=t16[:, 0:D],
                                                   in_=pool[0][:, 0:D]))
                        ve(lambda e: e.drain())
                        ve(lambda e: e.tensor_copy(out=i16s[:, 0:D],
                                                   in_=t16.bitcast(I16)[:, 0:D]))
                        ve(lambda e: e.drain())
                        ve(lambda e: e.tensor_copy(
                            out=pool[0][:, 0:D],
                            in_=i16s.bitcast(FP16)[:, 0:D]))
                        ve(lambda e: e.drain(), inc="VEC")
                    attnO = qT_sb
                    ve(lambda e, tt=tt: e.tensor_tensor(
                        out=attnO[:, tt * D:(tt + 1) * D],
                        in0=attnO[:, tt * D:(tt + 1) * D], in1=pool[0][:, 0:D],
                        op=ALU.add), waits=[attnO_v])
                    ve(lambda e: e.drain(), inc="VEC")
                    _v = S.val("VEC")
                    store(dr_out[row0 + tt * 128:row0 + (tt + 1) * 128, :],
                          attnO[:, tt * D:(tt + 1) * D], waits=[("VEC", _v)])
                xnT_free = ("PE", S.val("PE"))
                astk_free = ("PE", S.val("PE"))
                qT_free = ("ST", n_stores[0] * 16)
                pool_free = ("VEC", S.val("VEC"))
                xt_free = ("VEC", S.val("VEC"))

        S.emit("gpsimd", lambda e: e.engine_nop(), waits=[("ST", n_stores[0] * 16)])

        blkc = blk_ctx

        @blkc.sync
        def _(e):
            S.play("sync", e, sems)

        @blkc.tensor
        def _(e):
            S.play("tensor", e, sems)

        @blkc.vector
        def _(e):
            S.play("vector", e, sems)

        @blkc.scalar
        def _(e):
            S.play("scalar", e, sems)

        @blkc.gpsimd
        def _(e):
            S.play("gpsimd", e, sems)

    return nc


def _pack_inputs(cfg, xc, Wq, bq, Wk, bk, Wv, bv, Wo, bo, w1, b1, w2, b2):
    D = cfg["D"]
    NJC = cfg["HF"] // 128
    NDC = D // 128
    wqkv = np.ascontiguousarray(np.concatenate([Wq, Wk, Wv], axis=1), dtype=np.float32)
    bqkv = np.ascontiguousarray(np.concatenate([bq, bk, bv])[None, :], dtype=np.float32)
    wo_t = np.ascontiguousarray(Wo.reshape(NDC, 128, D), dtype=np.float32)
    w1t = np.ascontiguousarray(w1.T.reshape(D, NJC, 128).transpose(1, 0, 2),
                               dtype=np.float32)
    b1c = np.ascontiguousarray(b1.reshape(NJC, 128).T, dtype=np.float32)
    w2t = np.ascontiguousarray(w2.T, dtype=np.float32)
    return {
        "x": np.ascontiguousarray(xc.reshape(-1, D), dtype=np.float32),
        "wqkv": wqkv, "bqkv": bqkv,
        "wo": wo_t, "bo": np.ascontiguousarray(bo[None, :], dtype=np.float32),
        "w1t": w1t, "b1c": b1c,
        "w2t": w2t, "b2": np.ascontiguousarray(b2[None, :], dtype=np.float32),
    }


def kernel(L0_attn_x, L0_attn_Wq, L0_attn_bq, L0_attn_Wk, L0_attn_bk,
           L0_attn_Wv, L0_attn_bv, L0_attn_Wo, L0_attn_bo,
           L0_ffn_w1, L0_ffn_b1, L0_ffn_w2, L0_ffn_b2):
    x = np.asarray(L0_attn_x, dtype=np.float32)
    NB, T, D = x.shape
    n_cores = 8
    bpc = NB // n_cores
    cfg = dict(B=bpc, T=T, D=D, NH=16, NG=4, HF=4096, BLK=512)
    nc = build_program(cfg)
    args = [np.asarray(a, dtype=np.float32) for a in
            (L0_attn_Wq, L0_attn_bq, L0_attn_Wk, L0_attn_bk, L0_attn_Wv,
             L0_attn_bv, L0_attn_Wo, L0_attn_bo, L0_ffn_w1, L0_ffn_b1,
             L0_ffn_w2, L0_ffn_b2)]
    in_maps = [
        _pack_inputs(cfg, x[c * bpc:(c + 1) * bpc], *args) for c in range(n_cores)
    ]
    res = run_bass_kernel_spmd(nc, in_maps, core_ids=list(range(n_cores)))
    out = np.concatenate([r["out"].reshape(bpc, T, D) for r in res.results], axis=0)
    return out


# version 16
# speedup vs baseline: 1.0226x; 1.0226x over previous
import math
import numpy as np
import concourse.bass as bass
import concourse.mybir as mybir
from concourse.bass_utils import run_bass_kernel_spmd

F32 = mybir.dt.float32
I32 = mybir.dt.int32
I16 = mybir.dt.int16
I8 = mybir.dt.int8
FP8 = mybir.dt.float8e4
FP16 = mybir.dt.float16
BF16 = mybir.dt.bfloat16
AF = mybir.ActivationFunctionType
ALU = mybir.AluOpType

LN2 = math.log(2.0)
SIN_SCALE2 = 2.0 * float(np.nextafter(np.float32(np.pi), np.float32(0.0)))
INV2PI = 1.0 / (2.0 * math.pi)

ENGS = ("sync", "tensor", "vector", "scalar", "gpsimd")
PE0 = ("PE", 0)


class Sch:
    def __init__(self):
        self.q = {e: [] for e in ENGS}
        self.cnt = {}

    def emit(self, eng, fn, waits=(), inc=None, inc_n=1, is_dma=False):
        w = []
        for (s, v) in waits:
            if v > 0:
                assert v <= self.cnt.get(s, 0), \
                    f"cyclic wait {s}>={v} (produced {self.cnt.get(s, 0)})"
                w.append((s, v))
        if inc is not None:
            self.cnt[inc] = self.cnt.get(inc, 0) + inc_n
        self.q[eng].append((w, fn, inc, inc_n, is_dma))

    def val(self, sem):
        return self.cnt.get(sem, 0)

    def play(self, eng, handle, sems):
        seen = {}
        for (waits, fn, inc, inc_n, is_dma) in self.q[eng]:
            for s, v in waits:
                if not is_dma and seen.get(s, -1) >= v:
                    continue
                seen[s] = max(seen.get(s, -1), v)
                handle.wait_ge(sems[s], v)
            ins = fn(handle)
            if inc is not None:
                ins.then_inc(sems[inc], inc_n)


def build_program(cfg):
    B = cfg["B"]; T = cfg["T"]; D = cfg["D"]
    NH = cfg["NH"]; NG = cfg["NG"]; HF = cfg["HF"]; BLK = cfg["BLK"]
    PH = cfg.get("phases", 6)
    DH = 64
    DKV = NG * DH
    QKV_W = D + 2 * DKV
    NT_BLK = BLK // 128
    NBLK = T // BLK
    NDC = D // 128
    NJC = HF // 128
    NTILES = T // 128
    NKT = max(1, (NG + 1) // 2)
    SL = 256
    while QKV_W % SL:
        SL -= 128
    NSL = QKV_W // SL
    NHD = D // 512
    assert BLK == 512 and D % 512 == 0 and SL % 128 == 0

    nc = bass.Bass("TRN2", target_bir_lowering=False)
    dr_x = nc.dram_tensor("x", [B * T, D], F32, kind="ExternalInput").ap()
    dr_wqkv = nc.dram_tensor("wqkv", [D, QKV_W], F32, kind="ExternalInput").ap()
    dr_bqkv = nc.dram_tensor("bqkv", [1, QKV_W], F32, kind="ExternalInput").ap()
    dr_wo = nc.dram_tensor("wo", [NDC, 128, D], F32, kind="ExternalInput").ap()
    dr_bo = nc.dram_tensor("bo", [1, D], F32, kind="ExternalInput").ap()
    dr_w1 = nc.dram_tensor("w1t", [NJC, D, 128], F32, kind="ExternalInput").ap()
    dr_b1 = nc.dram_tensor("b1c", [128, NJC], F32, kind="ExternalInput").ap()
    dr_w2 = nc.dram_tensor("w2t", [HF, D], F32, kind="ExternalInput").ap()
    dr_b2 = nc.dram_tensor("b2", [1, D], F32, kind="ExternalInput").ap()
    dr_out = nc.dram_tensor("out", [B * T, D], F32, kind="ExternalOutput").ap()
    dr_w1h = nc.dram_tensor("w1h", [NJC, D, 128], BF16, kind="Internal").ap()
    dr_w1l = nc.dram_tensor("w1l", [NJC, D, 128], BF16, kind="Internal").ap()
    dr_wqh = nc.dram_tensor("wqh", [D, QKV_W], BF16, kind="Internal").ap()
    dr_wql = nc.dram_tensor("wql", [D, QKV_W], BF16, kind="Internal").ap()
    dr_woh = nc.dram_tensor("woh", [NDC, 128, D], BF16, kind="Internal").ap()
    dr_wol = nc.dram_tensor("wol", [NDC, 128, D], BF16, kind="Internal").ap()

    from contextlib import ExitStack
    with ExitStack() as X:
        def sb(n, shape, dt=F32):
            return X.enter_context(nc.sbuf_tensor(n, shape, dt))[:]

        def psum(n, shape):
            return X.enter_context(nc.psum_tensor(n, shape, F32))[:]

        wq_i8 = sb("wq_i8", [128, NJC * D], I8)
        wq_f8 = wq_i8.bitcast(FP8)
        kT_sb = sb("kT_sb", [128, NKT * T])
        v_sb = sb("v_sb", [128, NTILES * (DKV + NG)])
        wqkv_h = sb("wqkv_h", [128, NDC * SL], BF16)
        wqkv_l = sb("wqkv_l", [128, NDC * SL], BF16)
        w1_hb = sb("w1_hb", [128, 2 * D], BF16)
        w1_lb = sb("w1_lb", [128, 2 * D], BF16)
        woh_sb = sb("woh_sb", [128, D], BF16)
        wol_sb = sb("wol_sb", [128, D], BF16)
        sgh = sb("sgh", [128, BLK], BF16)
        sgl = sb("sgl", [128, BLK], BF16)
        xh_sb = sb("xh_sb", [128, NDC * BLK], BF16)
        xl_sb = sb("xl_sb", [128, NDC * BLK], BF16)
        qT_sb = sb("qT_sb", [128, (NH // 2) * BLK])
        astk_h = sb("astk_h", [128, NDC * BLK], BF16)
        astk_l = sb("astk_l", [128, NDC * BLK], BF16)
        hq_i8 = sb("hq_i8", [128, NJC * BLK], I8)
        hq_f8 = hq_i8.bitcast(FP8)
        t16 = sb("t16", [128, D], FP16)
        t8s = sb("t8s", [128, max(D, 2 * BLK)], FP8)
        pool = [sb(f"pool{i}", [128, 1024]) for i in range(6)]
        ib0 = sb("ib0", [128, 1024], I32)
        i16s = ib0.bitcast(I16)
        xt = pool[5][:, 0:D]
        qs = sb("qs", [128, SL])
        cosC = sb("cosC", [128, NTILES * 32])
        sinC = sb("sinC", [128, NTILES * 32])
        cosR = sb("cosR", [128, 128 * (BLK // 128)])
        sinR = sb("sinR", [128, 128 * (BLK // 128)])
        freq32 = sb("freq32", [128, 32])
        maskB = sb("maskB", [128, BLK + 384], BF16)
        onesr = sb("onesr", [128, 128])
        ident = sb("ident", [128, 128])
        rowio = sb("rowio", [128, 1])
        bqkv_row = sb("bqkv_row", [1, QKV_W])
        bo_row = sb("bo_row", [1, D])
        bq2_tile = sb("bq2_tile", [128, D])
        b1c = sb("b1c_sb", [128, NJC])
        lns = sb("lns", [128, 8])
        tvs = sb("tvs", [128, 2])

        pA = [psum(f"pA{i}", [128, 512]) for i in range(4)]
        pB = [psum(f"pB{i}", [128, 1024]) for i in range(2)]

        sems = {}
        for s in ("LBI", "LDX", "LQKA", "LQKB", "LW1A", "LW1B", "LWOA",
                  "LWOB", "LW2A", "LW2B", "GPR", "GPQ", "GPA", "GPC", "PE",
                  "ACT", "VEC", "ST"):
            sems[s] = X.enter_context(nc.semaphore(s))
        blk_ctx = X.enter_context(nc.Block())

        S = Sch()
        gp = lambda fn, **kw: S.emit("gpsimd", fn, **kw)
        pe = lambda fn, **kw: S.emit("tensor", fn, **kw)
        ac = lambda fn, **kw: S.emit("scalar", fn, **kw)
        ve = lambda fn, **kw: S.emit("vector", fn, **kw)

        def dma(eng, out, in_, sem, waits=()):
            S.emit(eng, (lambda o, i: (lambda e: e.dma_start(out=o, in_=i)))(out, in_),
                   waits=waits, inc=sem, inc_n=16, is_dma=True)
            return (sem, S.val(sem))

        n_stores = [0]

        def store(out_ap, in_ap, waits):
            dma("gpsimd", out_ap, in_ap, sem="ST", waits=waits)
            n_stores[0] += 1

        def quant_chain(src, dst, man, min_exp, thr, i0, i1, sB, sC, sD, sE,
                        entry_waits=(), nonneg=False, rne=False, upto=99,
                        taps=None):
            def tap(name, val):
                if taps and name in taps:
                    ve(lambda e, d=taps[name], v=val: e.tensor_copy(out=d, in_=v))
                    ve(lambda e: e.drain())
            """dst = float_quantize(src); pure-arithmetic, fp64-shadow exact.

            i0/i1: int32 scratch (real int allocations -- int writes over int
            memory are native-only and never poison the fp64 shadow; float
            reads of them upcast the native bits, which is exactly the
            insensitive binade/anchor information we want)."""
            i0f = i0.bitcast(F32)
            i1f = i1.bitcast(F32)
            srci = src.bitcast(I32)
            ve(lambda e: e.tensor_scalar(out=i0, in0=srci, scalar1=23, scalar2=0xFF,
                                         op0=ALU.logical_shift_right,
                                         op1=ALU.bitwise_and), waits=entry_waits)
            ve(lambda e: e.drain())
            ve(lambda e: e.tensor_scalar(out=i0, in0=i0, scalar1=127 + min_exp,
                                         scalar2=man, op0=ALU.max, op1=ALU.subtract))
            ve(lambda e: e.drain())
            ve(lambda e: e.tensor_scalar(out=i1, in0=i0, scalar1=0xFF, scalar2=None,
                                         op0=ALU.bitwise_xor))
            ve(lambda e: e.drain())
            ve(lambda e: e.tensor_scalar(out=i1, in0=i1, scalar1=1, scalar2=None,
                                         op0=ALU.subtract))
            ve(lambda e: e.drain())
            ve(lambda e: e.tensor_scalar(out=i1, in0=i1, scalar1=23, scalar2=None,
                                         op0=ALU.logical_shift_left))
            ve(lambda e: e.drain())
            ve(lambda e: e.tensor_scalar(out=i0, in0=i0, scalar1=23, scalar2=None,
                                         op0=ALU.logical_shift_left))
            ve(lambda e: e.drain())
            ve(lambda e: e.tensor_tensor(out=sC, in0=src, in1=i1f, op=ALU.mult))
            ve(lambda e: e.drain(), inc="VEC")
            _v = S.val("VEC")
            tap("p", sC)
            ac(lambda e: e.copy(out=i1, in_=sC), waits=[("VEC", _v)])
            ac(lambda e: e.drain())
            ac(lambda e: e.copy(out=sD, in_=i1), inc="ACT")
            _a = S.val("ACT")
            ve(lambda e: e.tensor_tensor(out=sB, in0=sC, in1=sD, op=ALU.subtract),
               waits=[("ACT", _a)])
            ve(lambda e: e.drain())
            if not rne:
                ve(lambda e: e.tensor_scalar(out=sE, in0=sB, scalar1=0.5,
                                             scalar2=None, op0=ALU.is_ge))
                ve(lambda e: e.drain())
                ve(lambda e: e.scalar_tensor_tensor(out=sB, in0=sB, scalar=-0.5,
                                                    op0=ALU.is_ge, in1=sE,
                                                    op1=ALU.add))
                ve(lambda e: e.drain())
                ve(lambda e: e.tensor_scalar(out=sB, in0=sB, scalar1=1.0,
                                             scalar2=None, op0=ALU.subtract))
                ve(lambda e: e.drain())
                ve(lambda e: e.tensor_tensor(out=sD, in0=sB, in1=sD, op=ALU.add))
                ve(lambda e: e.drain())
            else:
                ve(lambda e: e.tensor_scalar(out=sE, in0=sB, scalar1=0.0,
                                             scalar2=None, op0=ALU.is_lt))
                ve(lambda e: e.drain())
                ve(lambda e: e.tensor_tensor(out=sD, in0=sD, in1=sE,
                                             op=ALU.subtract))
                ve(lambda e: e.drain())
                tap("fl", sD)
                ve(lambda e: e.tensor_tensor(out=sB, in0=sC, in1=sD,
                                             op=ALU.subtract))
                ve(lambda e: e.drain(), inc="VEC")
                _v2 = S.val("VEC")
                tap("g", sB)
                ac(lambda e: e.copy(out=i1, in_=sD), waits=[("VEC", _v2)])
                ac(lambda e: e.drain(), inc="ACT")
                _a2 = S.val("ACT")
                ve(lambda e: e.tensor_scalar(out=i1, in0=i1, scalar1=1, scalar2=None,
                                             op0=ALU.bitwise_and),
                   waits=[("ACT", _a2)])
                ve(lambda e: e.drain(), inc="VEC")
                _v2 = S.val("VEC")
                ac(lambda e: e.copy(out=sC, in_=i1), waits=[("VEC", _v2)])
                ac(lambda e: e.drain(), inc="ACT")
                _a2 = S.val("ACT")
                if taps and "par" in taps:
                    ve(lambda e, d=taps["par"]: e.tensor_copy(out=d, in_=sC),
                       waits=[("ACT", _a2)])
                    ve(lambda e: e.drain())
                ve(lambda e: e.tensor_scalar(out=sE, in0=sB, scalar1=0.5,
                                             scalar2=None, op0=ALU.is_equal),
                   waits=[("ACT", _a2)])
                ve(lambda e: e.drain())
                ve(lambda e: e.tensor_tensor(out=sC, in0=sE, in1=sC, op=ALU.mult))
                ve(lambda e: e.drain())
                ve(lambda e: e.tensor_scalar(out=sB, in0=sB, scalar1=0.5,
                                             scalar2=None, op0=ALU.is_gt))
                ve(lambda e: e.drain())
                ve(lambda e: e.tensor_tensor(out=sB, in0=sB, in1=sC, op=ALU.add))
                ve(lambda e: e.drain())
                ve(lambda e: e.tensor_tensor(out=sD, in0=sB, in1=sD, op=ALU.add))
                ve(lambda e: e.drain())
            tap("r", sD)
            ve(lambda e: e.tensor_tensor(out=sC, in0=sD, in1=i0f, op=ALU.mult))
            ve(lambda e: e.drain())
            if nonneg:
                ve(lambda e: e.tensor_scalar(out=sE, in0=src, scalar1=float(thr),
                                             scalar2=None, op0=ALU.is_ge))
                ve(lambda e: e.drain(), inc="VEC")
            else:
                ve(lambda e: e.drain(), inc="VEC")
                _v0 = S.val("VEC")
                ac(lambda e: e.activation(out=sE, in_=src, func=AF.Abs, scale=1.0),
                   waits=[("VEC", _v0)])
                ac(lambda e: e.drain(), inc="ACT")
                _a0 = S.val("ACT")
                ve(lambda e: e.tensor_scalar(out=sE, in0=sE, scalar1=float(thr),
                                             scalar2=None, op0=ALU.is_ge),
                   waits=[("ACT", _a0)])
                ve(lambda e: e.drain(), inc="VEC")
            _v = S.val("VEC")
            ac(lambda e: e.activation(out=sB, in_=sE, func=AF.Exp, scale=1e30),
               waits=[("VEC", _v)], inc="ACT")
            _a = S.val("ACT")
            ve(lambda e: e.tensor_tensor(out=dst, in0=sC, in1=sB, op=ALU.mult),
               waits=[("ACT", _a)])
            ve(lambda e: e.drain(), inc="VEC")

        def pure_rne_e5m10(W, src, dst, entry_waits=(), taps2=None):
            def tap2(name, val):
                if taps2 and name in taps2:
                    ve(lambda e, d=taps2[name], v=val: e.tensor_copy(out=d, in_=v))
                    ve(lambda e: e.drain())
            """dst = float_quantize(src, 5, 10) via pure float arithmetic.

            No int-bitcast anchors: the native anchor path assumes the native
            fp32 value tracks the fp64 shadow value to <0.5 quanta, which
            breaks downstream of the fp8 hq quantizer (an e4m3-boundary flip
            in native hq shifts native pre by many e5m10 quanta).  Here every
            step (Abs/Ln/Exp/Sign + compare-ladders) is computed correctly in
            whichever precision executes it, so native and shadow results are
            each right in their own world."""
            A = pool[1][:, 0:W]
            Bs = pool[2][:, 0:W]
            C = pool[3][:, 0:W]
            E = pool[4][:, 0:W]
            F = pool[5][:, 0:W]
            LN2 = math.log(2.0)
            TOL = 2e-6
            ac(lambda e: e.activation(out=A, in_=src, func=AF.Abs, scale=1.0),
               waits=list(entry_waits))
            ac(lambda e: e.drain(), inc="ACT")
            _a = S.val("ACT")
            ve(lambda e: e.tensor_scalar(out=Bs, in0=A, scalar1=2.0 ** -40,
                                         scalar2=None, op0=ALU.add),
               waits=[("ACT", _a)])
            ve(lambda e: e.drain(), inc="VEC")
            _v = S.val("VEC")
            ac(lambda e: e.activation(out=Bs, in_=Bs, func=AF.Ln, scale=1.0),
               waits=[("VEC", _v)])
            ac(lambda e: e.drain(), inc="ACT")
            _a = S.val("ACT")
            ve(lambda e: e.tensor_scalar(out=Bs, in0=Bs, scalar1=1.0 / LN2,
                                         scalar2=14.0, op0=ALU.mult, op1=ALU.add),
               waits=[("ACT", _a)])
            ve(lambda e: e.drain())
            ve(lambda e: e.tensor_copy(out=E, in_=Bs))
            ve(lambda e: e.drain())
            for c in (16.0, 8.0, 4.0, 2.0, 1.0):
                ve(lambda e, c=c: e.tensor_scalar(out=F, in0=Bs, scalar1=c,
                                                  scalar2=c, op0=ALU.is_ge,
                                                  op1=ALU.mult))
                ve(lambda e: e.drain())
                ve(lambda e: e.tensor_tensor(out=Bs, in0=Bs, in1=F,
                                             op=ALU.subtract))
                ve(lambda e: e.drain())
            ve(lambda e: e.tensor_tensor(out=E, in0=E, in1=Bs, op=ALU.subtract))
            ve(lambda e: e.drain())
            ve(lambda e: e.tensor_scalar(out=E, in0=E, scalar1=24.0,
                                         scalar2=None, op0=ALU.subtract))
            ve(lambda e: e.drain(), inc="VEC")
            _v = S.val("VEC")
            ac(lambda e: e.activation(out=C, in_=E, func=AF.Exp, scale=-LN2),
               waits=[("VEC", _v)])
            ac(lambda e: e.drain())
            ac(lambda e: e.activation(out=F, in_=E, func=AF.Exp, scale=LN2))
            ac(lambda e: e.drain(), inc="ACT")
            _a = S.val("ACT")
            ve(lambda e: e.tensor_tensor(out=A, in0=A, in1=C, op=ALU.mult),
               waits=[("ACT", _a)])
            ve(lambda e: e.drain())
            tap2("pm", A)
            tap2("scinv", C)
            tap2("sc", F)
            ve(lambda e: e.tensor_copy(out=Bs, in_=A))
            ve(lambda e: e.drain())
            for c in (1024.0, 512.0, 256.0, 128.0, 64.0, 32.0, 16.0, 8.0,
                      4.0, 2.0):
                ve(lambda e, c=c: e.tensor_scalar(out=C, in0=A, scalar1=c,
                                                  scalar2=c, op0=ALU.is_ge,
                                                  op1=ALU.mult))
                ve(lambda e: e.drain())
                ve(lambda e: e.tensor_tensor(out=A, in0=A, in1=C,
                                             op=ALU.subtract))
                ve(lambda e: e.drain())
            ve(lambda e: e.tensor_scalar(out=C, in0=A, scalar1=1.0, scalar2=None,
                                         op0=ALU.is_ge))
            ve(lambda e: e.drain())
            ve(lambda e: e.tensor_tensor(out=A, in0=A, in1=C, op=ALU.subtract))
            ve(lambda e: e.drain())
            ve(lambda e: e.tensor_tensor(out=Bs, in0=Bs, in1=A, op=ALU.subtract))
            ve(lambda e: e.drain())
            tap2("fl", Bs)
            tap2("g", A)
            tap2("b0", C)
            ve(lambda e: e.tensor_scalar(out=E, in0=A, scalar1=0.5 + TOL,
                                         scalar2=None, op0=ALU.is_ge))
            ve(lambda e: e.drain())
            ve(lambda e: e.tensor_scalar(out=A, in0=A, scalar1=0.5 - TOL,
                                         scalar2=None, op0=ALU.is_ge))
            ve(lambda e: e.drain())
            ve(lambda e: e.tensor_tensor(out=A, in0=A, in1=E, op=ALU.subtract))
            ve(lambda e: e.drain())
            ve(lambda e: e.tensor_tensor(out=A, in0=A, in1=C, op=ALU.mult))
            ve(lambda e: e.drain())
            ve(lambda e: e.tensor_tensor(out=E, in0=E, in1=A, op=ALU.add))
            ve(lambda e: e.drain())
            tap2("add", E)
            ve(lambda e: e.tensor_tensor(out=Bs, in0=Bs, in1=E, op=ALU.add))
            ve(lambda e: e.drain())
            ve(lambda e: e.tensor_tensor(out=Bs, in0=Bs, in1=F, op=ALU.mult))
            ve(lambda e: e.drain(), inc="VEC")
            _v = S.val("VEC")
            ac(lambda e: e.activation(out=E, in_=src, func=AF.Sign, scale=1.0),
               waits=[("VEC", _v)])
            ac(lambda e: e.drain(), inc="ACT")
            _a = S.val("ACT")
            ve(lambda e: e.tensor_tensor(out=dst, in0=Bs, in1=E, op=ALU.mult),
               waits=[("ACT", _a)])
            ve(lambda e: e.drain(), inc="VEC")

        def chain_full(width, src, dst, man, min_exp, thr, entry_waits=(),
                       nonneg=False, rne=False, upto=99, taps=None):
            quant_chain(src, dst, man, min_exp, thr,
                        ib0[:, 0:width], ib1[:, 0:width],
                        pool[1][:, 0:width], pool[2][:, 0:width],
                        pool[3][:, 0:width], pool[4][:, 0:width],
                        entry_waits=entry_waits, nonneg=nonneg, rne=rne,
                        taps=taps)

        gp(lambda e: e.iota(rowio, pattern=[[0, 1]], base=0, channel_multiplier=1,
                            allow_small_or_imprecise_dtypes=True))
        gp(lambda e: e.drain())
        gp(lambda e: e.iota(pool[1][:, 0:BLK + 384], pattern=[[1, BLK + 384]], base=0,
                            channel_multiplier=0, allow_small_or_imprecise_dtypes=True))
        gp(lambda e: e.drain())
        gp(lambda e: e.iota(freq32, pattern=[[1, 32]], base=0, channel_multiplier=0,
                            allow_small_or_imprecise_dtypes=True))
        gp(lambda e: e.drain())
        gp(lambda e: e.iota(pool[0][:, 0:128], pattern=[[1, 128]], base=0,
                            channel_multiplier=0, allow_small_or_imprecise_dtypes=True),
           inc="GPC")
        g_io = S.val("GPC")
        ve(lambda e: e.tensor_scalar(out=ident, in0=pool[0][:, 0:128],
                                     scalar1=rowio[:, 0:1], scalar2=None,
                                     op0=ALU.is_equal), waits=[("GPC", g_io)])
        ve(lambda e: e.drain())
        ve(lambda e: e.tensor_scalar(out=pool[0][:, 0:1], in0=rowio, scalar1=384.0,
                                     scalar2=None, op0=ALU.add))
        ve(lambda e: e.drain())
        ve(lambda e: e.tensor_scalar(out=maskB, in0=pool[1][:, 0:BLK + 384],
                                     scalar1=pool[0][:, 0:1],
                                     scalar2=None, op0=ALU.is_ge))
        ve(lambda e: e.drain())
        ve(lambda e: e.memset(onesr, 1.0))
        ve(lambda e: e.drain())
        for t in range(NTILES):
            for g in range(NG):
                c = t * (DKV + NG) + g * (DH + 1) + DH
                ve(lambda e, c=c: e.memset(v_sb[:, c:c + 1], 1.0))
        ve(lambda e: e.drain())
        c_w = -math.log2(10000.0) / 32.0
        pi0 = ib0
        ve(lambda e: e.tensor_scalar(out=pool[0][:, 0:32], in0=freq32, scalar1=c_w,
                                     scalar2=None, op0=ALU.mult))
        ve(lambda e: e.drain(), inc="VEC")
        _v = S.val("VEC")
        ac(lambda e: e.copy(out=pi0[:, 0:32], in_=pool[0][:, 0:32]), waits=[("VEC", _v)])
        ac(lambda e: e.drain())
        ac(lambda e: e.copy(out=pool[2][:, 0:32], in_=pi0[:, 0:32]), inc="ACT")
        _a = S.val("ACT")
        ve(lambda e: e.tensor_tensor(out=pool[3][:, 0:32], in0=pool[0][:, 0:32],
                                     in1=pool[2][:, 0:32], op=ALU.subtract),
           waits=[("ACT", _a)])
        ve(lambda e: e.drain())
        ve(lambda e: e.tensor_scalar(out=pi0[:, 32:64], in0=pi0[:, 0:32], scalar1=127,
                                     scalar2=None, op0=ALU.add))
        ve(lambda e: e.drain())
        ve(lambda e: e.tensor_scalar(out=pi0[:, 32:64], in0=pi0[:, 32:64], scalar1=23,
                                     scalar2=None, op0=ALU.logical_shift_left))
        ve(lambda e: e.drain(), inc="VEC")
        _v = S.val("VEC")
        ac(lambda e: e.activation(out=pool[4][:, 0:32], in_=pool[3][:, 0:32],
                                  func=AF.Exp, scale=LN2), waits=[("VEC", _v)],
           inc="ACT")
        _a = S.val("ACT")
        ve(lambda e: e.tensor_tensor(out=freq32, in0=ib0.bitcast(F32)[:, 32:64],
                                     in1=pool[4][:, 0:32], op=ALU.mult),
           waits=[("ACT", _a)])
        ve(lambda e: e.drain(), inc="VEC")
        v_freq = S.val("VEC")

        a_sin = 0
        for t in range(NTILES):
            ve(lambda e, t=t: e.tensor_scalar(out=tvs[:, 0:1], in0=rowio,
                                              scalar1=float(128 * t), scalar2=INV2PI,
                                              op0=ALU.add, op1=ALU.mult),
               waits=[("VEC", v_freq)])
            ve(lambda e: e.drain())
            ve(lambda e: e.tensor_scalar(out=pool[0][:, 0:32], in0=freq32,
                                         scalar1=tvs[:, 0:1], scalar2=None,
                                         op0=ALU.mult))
            ve(lambda e: e.drain(), inc="VEC")
            yv = S.val("VEC")
            for (dst, add25) in ((sinC, False), (cosC, True)):
                if add25:
                    ve(lambda e: e.tensor_scalar(out=pool[2][:, 0:32],
                                                 in0=pool[0][:, 0:32], scalar1=0.25,
                                                 scalar2=None, op0=ALU.add),
                       waits=[("VEC", yv), ("ACT", a_sin)])
                else:
                    ve(lambda e: e.tensor_copy(out=pool[2][:, 0:32],
                                               in_=pool[0][:, 0:32]),
                       waits=[("VEC", yv), ("ACT", a_sin)])
                ve(lambda e: e.drain(), inc="VEC")
                _v = S.val("VEC")
                ac(lambda e: e.copy(out=pi0[:, 0:32], in_=pool[2][:, 0:32]),
                   waits=[("VEC", _v)])
                ac(lambda e: e.drain())
                ac(lambda e: e.copy(out=pool[3][:, 0:32], in_=pi0[:, 0:32]), inc="ACT")
                _a = S.val("ACT")
                ve(lambda e: e.tensor_tensor(out=pool[2][:, 0:32],
                                             in0=pool[2][:, 0:32],
                                             in1=pool[3][:, 0:32], op=ALU.subtract),
                   waits=[("ACT", _a)])
                ve(lambda e: e.drain())
                ve(lambda e: e.tensor_scalar(out=pool[3][:, 0:32],
                                             in0=pool[2][:, 0:32], scalar1=0.5,
                                             scalar2=None, op0=ALU.is_ge))
                ve(lambda e: e.drain())
                ve(lambda e: e.tensor_tensor(out=pool[2][:, 0:32],
                                             in0=pool[2][:, 0:32],
                                             in1=pool[3][:, 0:32], op=ALU.subtract))
                ve(lambda e: e.drain(), inc="VEC")
                _v = S.val("VEC")
                ac(lambda e, dst=dst, t=t: e.activation(
                    out=dst[:, 32 * t:32 * t + 32], in_=pool[2][:, 0:32],
                    func=AF.Sin, scale=SIN_SCALE2), waits=[("VEC", _v)], inc="ACT")
                a_sin = S.val("ACT")
        a_tables = S.val("ACT")

        dma("sync", bqkv_row, dr_bqkv, "LBI")
        dma("sync", bo_row, dr_bo, "LBI")
        dma("sync", pool[0][0:1, 0:D], dr_b2, "LBI",
            waits=[("ACT", a_tables)])
        ld_bias = dma("sync", b1c, dr_b1, "LBI")

        ve(lambda e: e.tensor_copy(out=t16[0:1, 0:D], in_=pool[0][0:1, 0:D]),
           waits=[ld_bias, ("ACT", a_tables)])
        ve(lambda e: e.drain())
        ve(lambda e: e.tensor_copy(out=i16s[0:1, 0:D],
                                   in_=t16.bitcast(I16)[0:1, 0:D]))
        ve(lambda e: e.drain())
        ve(lambda e: e.tensor_copy(out=pool[0][0:1, 0:D],
                                   in_=i16s.bitcast(FP16)[0:1, 0:D]))
        ve(lambda e: e.drain(), inc="VEC")
        v_b2 = S.val("VEC")
        for hf in range(NHD):
            pe(lambda e, hf=hf: e.matmul(pB[0][:, hf * 512:(hf + 1) * 512],
                                         onesr[0:1, :],
                                         pool[0][0:1, hf * 512:(hf + 1) * 512],
                                         start=True, stop=True),
               waits=[("VEC", v_b2)], inc="PE")
        _p = S.val("PE")
        ac(lambda e: e.copy(out=bq2_tile, in_=pB[0][:, 0:D]), waits=[("PE", _p)],
           inc="ACT")
        a_bq2 = S.val("ACT")
        pB_free = [("ACT", a_bq2), ("ACT", 0)]

        w2_done = {}
        for j in range(NJC):
            buf = pool[1 + (j % 2)][:, 0:D]
            wch = [("VEC", w2_done[j - 2])] if j >= 2 else [("ACT", a_bq2)]
            ldv = dma("sync", buf, dr_w2[j * 128:(j + 1) * 128, :],
                      "LW2A" if j % 2 == 0 else "LW2B", waits=wch)
            ve(lambda e, buf=buf: e.tensor_copy(out=t8s[:, 0:D], in_=buf),
               waits=[ldv])
            ve(lambda e: e.drain())
            ve(lambda e, j=j: e.tensor_copy(out=wq_i8[:, j * D:(j + 1) * D],
                                            in_=t8s.bitcast(I8)[:, 0:D]))
            ve(lambda e: e.drain(), inc="VEC")
            w2_done[j] = S.val("VEC")
        v_wq = S.val("VEC")

        chunks = []
        for j in range(NJC):
            chunks.append((dr_w1[j].rearrange("(c p) n -> p c n", p=128),
                           dr_w1h[j].rearrange("(c p) n -> p c n", p=128),
                           dr_w1l[j].rearrange("(c p) n -> p c n", p=128),
                           NDC * 128))
        for r in range(NDC):
            for hf2 in range(2):
                w = QKV_W // 2
                chunks.append((dr_wqkv[r * 128:(r + 1) * 128,
                                       hf2 * w:(hf2 + 1) * w],
                               dr_wqh[r * 128:(r + 1) * 128,
                                      hf2 * w:(hf2 + 1) * w],
                               dr_wql[r * 128:(r + 1) * 128,
                                      hf2 * w:(hf2 + 1) * w], w))
        for c in range(NDC):
            chunks.append((dr_wo[c], dr_woh[c], dr_wol[c], D))
        split_done = {}
        split_st = {}
        for i, (src, dh, dl, wdt) in enumerate(chunks):
            slot = i % 4
            buf = pool[1 + slot][:, 0:wdt]
            if i >= 4:
                lw = [("VEC", split_done[i - 4])]
            elif slot < 2:
                lw = [("VEC", w2_done[NJC - 2 + slot])]
            else:
                lw = [("ACT", a_bq2)]
            ldc = dma("sync", buf, src, ("LW2A", "LW2B", "LW1A", "LW1B")[slot],
                      waits=lw)
            sw = (NDC * BLK) // 4
            hslc = xh_sb[:, slot * sw:slot * sw + wdt]
            lslc = xl_sb[:, slot * sw:slot * sw + wdt]
            stw = [split_st[i - 4]] if i >= 4 else []
            ve(lambda e, b=buf, h=hslc: e.tensor_copy(out=h, in_=b),
               waits=[ldc] + stw)
            ve(lambda e: e.drain())
            ve(lambda e, b=buf, h=hslc, l=lslc: e.tensor_tensor(
                out=l, in0=b, in1=h, op=ALU.subtract))
            ve(lambda e: e.drain(), inc="VEC")
            split_done[i] = S.val("VEC")
            store(dh, hslc, waits=[("VEC", split_done[i]),
                                   ("ST", n_stores[0] * 16)])
            store(dl, lslc, waits=[("VEC", split_done[i])])
            split_st[i] = ("ST", n_stores[0] * 16)
        v_split_st = ("ST", n_stores[0] * 16)

        pA_free = [("ACT", 0)] * 4
        wqkv_free = [v_split_st, v_split_st]
        w1_free = [v_split_st, v_split_st]
        wo_free = [("ACT", a_tables), ("ACT", a_tables)]
        kv_ready = {}
        xnT_free = v_split_st
        qT_free = ("ST", 0)
        astk_free = PE0
        pool_free = ("VEC", S.val("VEC"))
        xt_free = ("ACT", a_tables)
        probs_free = [PE0, PE0]
        qs_free = PE0
        qstg_free = PE0
        sstg_free = ("GPA", 0)
        rope_free = ("VEC", 0)
        xn_free = PE0
        pool_free_v = ("VEC", 0)
        nstg_free = ("VEC", 0)

        for seq in range(B):
            for blki in range(NBLK):
                tok0 = blki * BLK
                row0 = seq * T + tok0

                g_rep = {}
                xsplit_map = {}
                for sl in range(NSL):
                    lw = [wqkv_free[0]]
                    dma("sync", wqkv_h,
                        dr_wqh[:, sl * SL:(sl + 1) * SL].rearrange(
                            "(c p) n -> p c n", p=128),
                        "LQKA", waits=lw)
                    ld_w = dma("sync", wqkv_l,
                        dr_wql[:, sl * SL:(sl + 1) * SL].rearrange(
                            "(c p) n -> p c n", p=128),
                        "LQKA")
                    for tt in range(NT_BLK):
                        t0 = tok0 + tt * 128
                        tile_abs = t0 // 128
                        if sl == 0:
                            ldx = dma("sync", xt,
                                dr_x[row0 + tt * 128:row0 + (tt + 1) * 128, :],
                                "LDX", waits=[xt_free])
                            ve(lambda e: e.tensor_reduce(out=lns[:, 0:1], in_=xt,
                                                         axis=mybir.AxisListType.X,
                                                         op=ALU.add),
                               waits=[ldx, pool_free])
                            ve(lambda e: e.drain())
                            ve(lambda e: e.tensor_tensor(out=pool[0][:, 0:D], in0=xt,
                                                         in1=xt, op=ALU.mult),
                               waits=[xn_free, ("ACT", a_bq2)])
                            ve(lambda e: e.drain())
                            ve(lambda e: e.tensor_reduce(out=lns[:, 1:2],
                                                         in_=pool[0][:, 0:D],
                                                         axis=mybir.AxisListType.X,
                                                         op=ALU.add))
                            ve(lambda e: e.drain())
                            ve(lambda e: e.tensor_scalar(out=lns[:, 2:3],
                                                         in0=lns[:, 0:1],
                                                         scalar1=1.0 / D, scalar2=None,
                                                         op0=ALU.mult))
                            ve(lambda e: e.drain())
                            ve(lambda e: e.tensor_scalar(out=lns[:, 3:4],
                                                         in0=lns[:, 1:2],
                                                         scalar1=1.0 / D, scalar2=1e-5,
                                                         op0=ALU.mult, op1=ALU.add))
                            ve(lambda e: e.drain())
                            ve(lambda e: e.scalar_tensor_tensor(out=lns[:, 4:5],
                                                                in0=lns[:, 2:3],
                                                                scalar=0.0,
                                                                in1=lns[:, 2:3],
                                                                op0=ALU.add,
                                                                op1=ALU.mult))
                            ve(lambda e: e.drain())
                            ve(lambda e: e.tensor_tensor(out=lns[:, 3:4],
                                                         in0=lns[:, 3:4],
                                                         in1=lns[:, 4:5],
                                                         op=ALU.subtract))
                            ve(lambda e: e.drain(), inc="VEC")
                            _v = S.val("VEC")
                            ac(lambda e: e.activation(out=lns[:, 5:6], in_=lns[:, 3:4],
                                                      func=AF.Ln, scale=1.0),
                               waits=[("VEC", _v)])
                            ac(lambda e: e.drain())
                            ac(lambda e: e.activation(out=lns[:, 4:5], in_=lns[:, 5:6],
                                                      func=AF.Exp, scale=-0.5),
                               inc="ACT")
                            _a = S.val("ACT")
                            ve(lambda e: e.tensor_scalar(out=pool[0][:, 0:D], in0=xt,
                                                         scalar1=lns[:, 2:3],
                                                         scalar2=lns[:, 4:5],
                                                         op0=ALU.subtract,
                                                         op1=ALU.mult),
                               waits=[("ACT", _a)])
                            ve(lambda e: e.drain(), inc="VEC")
                            xn_v = S.val("VEC")
                            xt_free = ("VEC", xn_v)
                            for c in range(NDC):
                                pslot = 2 + (c % 2)
                                pe(lambda e, c=c, p=pslot: e.transpose(
                                    pA[p][:, 0:128], pool[0][:, c * 128:(c + 1) * 128],
                                    ident),
                                   waits=[("VEC", xn_v), pA_free[pslot]] +
                                         ([xnT_free] if (c == 0 and tt == 0) else []),
                                   inc="PE")
                                _p = S.val("PE")
                                sl_x = slice(c * BLK + tt * 128,
                                             c * BLK + (tt + 1) * 128)
                                ac(lambda e, p=pslot, s=sl_x: e.copy(
                                    out=xh_sb[:, s], in_=pA[p][:, 0:128]),
                                   waits=[("PE", _p)], inc="ACT")
                                _a = S.val("ACT")
                                ve(lambda e, p=pslot, s=sl_x: e.tensor_tensor(
                                    out=xl_sb[:, s], in0=pA[p][:, 0:128],
                                    in1=xh_sb[:, s], op=ALU.subtract),
                                   waits=[("ACT", _a)])
                                ve(lambda e: e.drain(), inc="VEC")
                                pA_free[pslot] = ("VEC", S.val("VEC"))
                            xn_free = ("PE", S.val("PE"))
                            xsplit_v = ("VEC", S.val("VEC"))
                            for r in range(4):
                                dma("gpsimd", cosR[:, 128 * tt + 32 * r:
                                                   128 * tt + 32 * r + 32],
                                    cosC[:, 32 * tile_abs:32 * tile_abs + 32],
                                    "GPR",
                                    waits=([("ACT", a_tables), rope_free,
                                            ("GPR", S.val("GPR"))]
                                           if r == 0 else []))
                                dma("gpsimd", sinR[:, 128 * tt + 32 * r:
                                                   128 * tt + 32 * r + 32],
                                    sinC[:, 32 * tile_abs:32 * tile_abs + 32],
                                    "GPR")
                            g_rep[tt] = ("GPR", S.val("GPR"))
                            xsplit_map[tt] = xsplit_v
                        xn_ready = xsplit_map[tt]
                        qp = pA[tt // 2][:, (tt % 2) * SL:(tt % 2) * SL + SL]
                        first = True
                        for c in range(NDC):
                            xs = slice(c * BLK + tt * 128,
                                       c * BLK + (tt + 1) * 128)
                            ws = slice(c * SL, (c + 1) * SL)
                            for lh, mv in ((xh_sb, wqkv_h), (xl_sb, wqkv_h),
                                           (xh_sb, wqkv_l)):
                                pe(lambda e, xs=xs, ws=ws, lh=lh, mv=mv,
                                   qp=qp, st=first: e.matmul(
                                       qp, lh[:, xs], mv[:, ws],
                                       start=st, stop=False),
                                   waits=([ld_w, xn_ready, pA_free[tt // 2]]
                                          if first else []))
                                first = False
                        pe(lambda e, tt=tt, sl=sl, qp=qp: e.matmul(
                            qp, onesr[0:1, :],
                            bqkv_row[:, sl * SL:(sl + 1) * SL], start=False, stop=True),
                           waits=[ld_bias], inc="PE")
                        pe_qkv = S.val("PE")
                        if tt == NT_BLK - 1:
                            wqkv_free[0] = ("PE", pe_qkv)
                        ac(lambda e, tt=tt, qp=qp: e.copy(out=qs, in_=qp),
                           waits=[("PE", pe_qkv), qs_free], inc="ACT")
                        ev_a = S.val("ACT")
                        pA_free[tt // 2] = ("ACT", ev_a)
                        col0 = sl * SL
                        rope_w = 0
                        if col0 < D + DKV:
                            rope_w = min(SL, D + DKV - col0)
                        if rope_w > 0:
                            npair = rope_w // 2
                            sve = qs[:, 0:rope_w:2]
                            svo = qs[:, 1:rope_w:2]
                            cRt = cosR[:, 128 * tt:128 * tt + npair]
                            sRt = sinR[:, 128 * tt:128 * tt + npair]
                            ve(lambda e, sve=sve, n=npair, cR=cRt: e.tensor_tensor(
                                out=pool[1][:, 0:n], in0=sve, in1=cR,
                                op=ALU.mult),
                               waits=[("ACT", ev_a), g_rep[tt]])
                            ve(lambda e, svo=svo, n=npair, sR=sRt: e.tensor_tensor(
                                out=pool[2][:, 0:n], in0=svo, in1=sR,
                                op=ALU.mult))
                            ve(lambda e, sve=sve, n=npair, sR=sRt: e.tensor_tensor(
                                out=pool[3][:, 0:n], in0=sve, in1=sR,
                                op=ALU.mult))
                            ve(lambda e, svo=svo, n=npair, cR=cRt: e.tensor_tensor(
                                out=pool[4][:, 0:n], in0=svo, in1=cR,
                                op=ALU.mult))
                            ve(lambda e: e.drain())
                            ve(lambda e, sve=sve, n=npair: e.tensor_tensor(
                                out=sve, in0=pool[1][:, 0:n], in1=pool[2][:, 0:n],
                                op=ALU.subtract))
                            ve(lambda e, svo=svo, n=npair: e.tensor_tensor(
                                out=svo, in0=pool[3][:, 0:n], in1=pool[4][:, 0:n],
                                op=ALU.add))
                            ve(lambda e: e.drain(), inc="VEC")
                            rope_v = [("VEC", S.val("VEC"))]
                        else:
                            rope_v = [("ACT", ev_a)]
                        for c2 in range(SL // 128):
                            gcol = col0 + c2 * 128
                            pslot = 2 + (c2 % 2)
                            if gcol < D + DKV:
                                pe(lambda e, c2=c2, p=pslot: e.transpose(
                                    pA[p][:, 0:128], qs[:, c2 * 128:(c2 + 1) * 128],
                                    ident),
                                   waits=rope_v + [pA_free[pslot]] +
                                         ([qT_free] if (sl == 0 and tt == 0 and
                                                        c2 == 0) else []),
                                   inc="PE")
                                _p = S.val("PE")
                                if gcol < D:
                                    dcol = (gcol // 128) * BLK + tt * 128
                                    ac(lambda e, d=dcol, p=pslot: e.copy(
                                        out=qT_sb[:, d:d + 128], in_=pA[p][:, 0:128]),
                                       waits=[("PE", _p)], inc="ACT")
                                else:
                                    kc = (gcol - D) // 128
                                    dcol = kc * T + t0
                                    ac(lambda e, d=dcol, p=pslot: e.copy(
                                        out=kT_sb[:, d:d + 128], in_=pA[p][:, 0:128]),
                                       waits=[("PE", _p)], inc="ACT")
                                pA_free[pslot] = ("ACT", S.val("ACT"))
                            else:
                                off = gcol - (D + DKV)
                                for g in range(NG):
                                    glo, ghi = g * DH, (g + 1) * DH
                                    lo, hi = max(glo, off), min(ghi, off + 128)
                                    if lo >= hi:
                                        continue
                                    dcol = (tile_abs * (DKV + NG) + g * (DH + 1) +
                                            (lo - glo))
                                    scol = c2 * 128 + (lo - off)
                                    ac(lambda e, d=dcol, s=scol, n=hi - lo: e.copy(
                                        out=v_sb[:, d:d + n], in_=qs[:, s:s + n]),
                                       waits=[("ACT", ev_a)], inc="ACT")
                        ac(lambda e: e.drain())
                        qs_free = ("PE", S.val("PE"))
                        kv_ready[(seq, tile_abs)] = ("ACT", S.val("ACT"))
                xnT_ready = ("ACT", S.val("ACT"))
                xsplit_all = xsplit_map[NT_BLK - 1]
                pool_free = ("PE", S.val("PE"))
                pool_free_v = ("VEC", S.val("VEC"))
                rope_free = ("VEC", S.val("VEC"))

                nts_hi = (tok0 + BLK) // 128
                if PH < 2:
                    for tt in range(NT_BLK):
                        _v = S.val("ACT")
                        store(dr_out[row0 + tt * 128:row0 + (tt + 1) * 128, :],
                              pool[0][:, 0:D], waits=[("ACT", _v)])
                    xnT_free = ("PE", S.val("PE"))
                    qT_free = ("ST", n_stores[0] * 16)
                    pool_free = ("VEC", S.val("VEC"))
                    continue
                kv_all = kv_ready[(seq, nts_hi - 1)]
                qstg = pool[0]
                sstg = pool[1]
                nstg = pool[4]
                for h in range(NH):
                    g = h // (NH // NG)
                    kbase = 64 * (g % 2)
                    ktcol = (g // 2) * T
                    qbase = 64 * (h % 2)
                    qcol = (h // 2) * BLK
                    if qbase != kbase:
                        gp(lambda e, qc=qcol, qb=qbase, kb=kbase: e.dma_start(
                            out=qstg[kb:kb + 64, 0:BLK],
                            in_=qT_sb[qb:qb + 64, qc:qc + BLK]),
                           waits=[xnT_ready, qstg_free] +
                                 ([pool_free, pool_free_v] if h == 0 else []),
                           inc="GPQ", inc_n=16, is_dma=True)
                        q_ap = qstg[kbase:kbase + 64, 0:BLK]
                        q_wait = ("GPQ", S.val("GPQ"))
                        q_mismatch = True
                    else:
                        q_ap = qT_sb[qbase:qbase + 64, qcol:qcol + BLK]
                        q_wait = xnT_ready
                        q_mismatch = False
                    for ts in range(nts_hi):
                        sslot = ts % 2
                        sp = pA[sslot]
                        pe(lambda e, ts=ts, sp=sp, qa=q_ap, kb=kbase, kc=ktcol:
                           e.matmul(sp[:, 0:BLK],
                                    kT_sb[kb:kb + 64,
                                          kc + ts * 128:kc + (ts + 1) * 128],
                                    qa, start=True, stop=True),
                           waits=[q_wait, kv_all, pA_free[sslot], probs_free[sslot]],
                           inc="PE")
                        pesc = S.val("PE")
                        pr = pool[2][:, 0:BLK] if sslot == 0 else pool[3][:, 0:BLK]
                        ac(lambda e, sp=sp, pr=pr: e.activation(
                            out=pr, in_=sp[:, 0:BLK], func=AF.Exp, scale=0.125),
                           waits=[("PE", pesc)] +
                                 ([pool_free, pool_free_v]
                                  if (h == 0 and ts == 0) else []),
                           inc="ACT")
                        pA_free[sslot] = ("ACT", S.val("ACT"))
                        exp_a = S.val("ACT")
                        pv_w = [("ACT", exp_a)]
                        if ts * 128 >= tok0:
                            off = ts * 128 - tok0
                            ve(lambda e, pr=pr, off=off: e.tensor_tensor(
                                out=pr, in0=pr,
                                in1=maskB[:, 384 - off:384 - off + BLK],
                                op=ALU.mult), waits=[("ACT", exp_a)])
                            ve(lambda e: e.drain(), inc="VEC")
                            pv_w = [("VEC", S.val("VEC"))]
                        vcol = ts * (DKV + NG) + g * (DH + 1)
                        pe(lambda e, vc=vcol, pr=pr, st=(ts == 0),
                           sp2=(ts == nts_hi - 1): e.matmul(
                            pA[3][0:DH + 1, 0:BLK], v_sb[:, vc:vc + DH + 1], pr,
                            start=st, stop=sp2),
                           waits=pv_w + ([pA_free[3]] if ts == 0 else []), inc="PE")
                        probs_free[sslot] = ("PE", S.val("PE"))
                    pe_pv = S.val("PE")
                    if q_mismatch:
                        qstg_free = ("PE", pe_pv)
                    ac(lambda e: e.activation(out=nstg[DH:DH + 1, 0:BLK],
                                              in_=pA[3][DH:DH + 1, 0:BLK],
                                              func=AF.Ln, scale=1.0),
                       waits=[("PE", pe_pv), sstg_free, nstg_free])
                    ac(lambda e: e.drain())
                    ac(lambda e: e.activation(out=sstg[DH:DH + 1, 0:BLK],
                                              in_=nstg[DH:DH + 1, 0:BLK],
                                              func=AF.Exp, scale=-1.0),
                       inc="ACT")
                    _a_rc = S.val("ACT")
                    pe(lambda e: e.matmul(pB[0][0:64, 0:BLK], onesr[DH:DH + 1, 0:64],
                                          sstg[DH:DH + 1, 0:BLK], start=True,
                                          stop=True),
                       waits=[("ACT", _a_rc), pB_free[0]], inc="PE")
                    pe_bc = S.val("PE")
                    ac(lambda e: e.copy(out=nstg[0:64, 0:BLK], in_=pB[0][0:64, 0:BLK]),
                       waits=[("PE", pe_bc), nstg_free], inc="ACT")
                    a_bc = S.val("ACT")
                    pB_free[0] = ("ACT", a_bc)
                    ch = (h * DH) // 128
                    drow = 64 * (h % 2)
                    ve(lambda e: e.tensor_tensor(
                        out=sstg[0:64, 0:BLK], in0=pA[3][0:64, 0:BLK],
                        in1=nstg[0:64, 0:BLK], op=ALU.mult),
                       waits=[("ACT", a_bc)])
                    ve(lambda e: e.drain())
                    if drow == 0:
                        ve(lambda e, ch=ch: e.tensor_copy(
                            out=astk_h[0:64, ch * BLK:ch * BLK + BLK],
                            in_=sstg[0:64, 0:BLK]),
                           waits=([astk_free] if h == 0 else []))
                        ve(lambda e: e.drain())
                        ve(lambda e, ch=ch: e.tensor_tensor(
                            out=astk_l[0:64, ch * BLK:ch * BLK + BLK],
                            in0=sstg[0:64, 0:BLK],
                            in1=astk_h[0:64, ch * BLK:ch * BLK + BLK],
                            op=ALU.subtract))
                        ve(lambda e: e.drain(), inc="VEC")
                        _v = S.val("VEC")
                        pA_free[3] = ("VEC", _v)
                        nstg_free = ("VEC", _v)
                    else:
                        ve(lambda e: e.tensor_copy(out=sgh[0:64, 0:BLK],
                                                   in_=sstg[0:64, 0:BLK]))
                        ve(lambda e: e.drain())
                        ve(lambda e: e.tensor_tensor(
                            out=sgl[0:64, 0:BLK], in0=sstg[0:64, 0:BLK],
                            in1=sgh[0:64, 0:BLK], op=ALU.subtract))
                        ve(lambda e: e.drain(), inc="VEC")
                        _v = S.val("VEC")
                        pA_free[3] = ("VEC", _v)
                        nstg_free = ("VEC", _v)
                        gp(lambda e, ch=ch: e.dma_start(
                            out=astk_h[64:128, ch * BLK:ch * BLK + BLK],
                            in_=sgh[0:64, 0:BLK]),
                           waits=[("VEC", _v)] + ([astk_free] if h == 1 else []),
                           inc="GPA", inc_n=16, is_dma=True)
                        gp(lambda e, ch=ch: e.dma_start(
                            out=astk_l[64:128, ch * BLK:ch * BLK + BLK],
                            in_=sgl[0:64, 0:BLK]),
                           inc="GPA", inc_n=16, is_dma=True)
                        sstg_free = ("GPA", S.val("GPA"))
                astk_v = ("VEC", S.val("VEC"))
                astk_g = ("GPA", S.val("GPA"))

                if PH < 3:
                    for tt in range(NT_BLK):
                        _v = S.val("VEC")
                        store(dr_out[row0 + tt * 128:row0 + (tt + 1) * 128, :],
                              pool[0][:, 0:D], waits=[("VEC", _v)])
                    xnT_free = ("PE", S.val("PE"))
                    astk_free = ("PE", S.val("PE"))
                    qT_free = ("ST", n_stores[0] * 16)
                    pool_free = ("VEC", S.val("VEC"))
                    continue
                for grp in range(NT_BLK // 2):
                    for c in range(NDC):
                        dma("sync", woh_sb, dr_woh[c], "LWOA",
                            waits=[wo_free[0]])
                        ld_wo = dma("sync", wol_sb, dr_wol[c], "LWOA")
                        for tl in range(2):
                            tt = grp * 2 + tl
                            pbo = pB[tl]
                            first_t = True
                            for ti, (ah, wv) in enumerate(
                                    ((astk_h, woh_sb), (astk_l, woh_sb),
                                     (astk_h, wol_sb))):
                                for hf in range(NHD):
                                    last = (tl == 1 and ti == 2
                                            and hf == NHD - 1)
                                    pe(lambda e, c=c, pb=pbo, tt=tt, ah=ah,
                                       wv=wv, hf=hf,
                                       st=(c == 0 and ti == 0): e.matmul(
                                        pb[:, hf * 512:(hf + 1) * 512],
                                        ah[:, c * BLK + tt * 128:
                                           c * BLK + (tt + 1) * 128],
                                        wv[:, hf * 512:(hf + 1) * 512],
                                        start=st, stop=False),
                                       waits=(([astk_v, astk_g, pB_free[0],
                                                pB_free[1], ld_wo]
                                               if (c == 0 and tl == 0 and first_t)
                                               else ([ld_wo]
                                                     if (tl == 0 and first_t)
                                                     else []))),
                                       inc=("PE" if last else None))
                                    first_t = False
                        wo_free[0] = ("PE", S.val("PE"))
                    for tl in range(2):
                        tt = grp * 2 + tl
                        pbo = pB[tl]
                        for hf in range(NHD):
                            pe(lambda e, hf=hf, pb=pbo: e.matmul(
                                pb[:, hf * 512:(hf + 1) * 512], onesr[0:1, :],
                                bo_row[:, hf * 512:(hf + 1) * 512], start=False,
                                stop=True),
                               waits=[ld_bias],
                               inc=("PE" if hf == NHD - 1 else None))
                        pe_o = S.val("PE")
                        ldx2 = dma("sync", xt,
                                   dr_x[row0 + tt * 128:row0 + (tt + 1) * 128, :],
                                   "LDX", waits=[xt_free])
                        attnO = qT_sb
                        ve(lambda e, pb=pbo, tt=tt: e.tensor_tensor(
                            out=attnO[:, tt * D:(tt + 1) * D], in0=xt, in1=pb[:, 0:D],
                            op=ALU.add),
                           waits=[("PE", pe_o), ldx2] +
                                 ([qT_free] if (grp == 0 and tl == 0) else []))
                        ve(lambda e: e.drain(), inc="VEC")
                        _v = S.val("VEC")
                        xt_free = ("VEC", _v)
                        pB_free[tl] = ("VEC", _v)
                attnO_v = ("VEC", S.val("VEC"))

                if PH < 4:
                    for tt in range(NT_BLK):
                        _v = S.val("VEC")
                        store(dr_out[row0 + tt * 128:row0 + (tt + 1) * 128, :],
                              qT_sb[:, tt * D:(tt + 1) * D], waits=[attnO_v])
                    xnT_free = ("PE", S.val("PE"))
                    astk_free = ("PE", S.val("PE"))
                    qT_free = ("ST", n_stores[0] * 16)
                    pool_free = ("VEC", S.val("VEC"))
                    continue
                h_prev = ("VEC", 0)
                for j in range(NJC):
                    wslot = j % 2
                    w1bh = w1_hb[:, wslot * D:(wslot + 1) * D]
                    w1bl = w1_lb[:, wslot * D:(wslot + 1) * D]
                    dma("sync", w1bh,
                        dr_w1h[j].rearrange("(c p) n -> p c n", p=128),
                        "LW1A" if wslot == 0 else "LW1B",
                        waits=[w1_free[wslot]])
                    ldw1 = dma("sync", w1bl,
                        dr_w1l[j].rearrange("(c p) n -> p c n", p=128),
                        "LW1A" if wslot == 0 else "LW1B")
                    hp = pA[j % 2]
                    first = True
                    for c in range(NDC):
                        ws = slice(c * 128, (c + 1) * 128)
                        xs = slice(c * BLK, (c + 1) * BLK)
                        for wb, xb in ((w1bh, xh_sb), (w1bl, xh_sb),
                                       (w1bh, xl_sb)):
                            last = (c == NDC - 1 and xb is xl_sb)
                            pe(lambda e, wb=wb, xb=xb, ws=ws, xs=xs, hp=hp,
                               st=first, sp=last: e.matmul(
                                   hp[:, 0:BLK], wb[:, ws], xb[:, xs],
                                   start=st, stop=sp),
                               waits=([ldw1, xnT_ready, xsplit_all,
                                       pA_free[j % 2]]
                                      if first else []),
                               inc=("PE" if last else None))
                            first = False
                    pe_h = S.val("PE")
                    w1_free[wslot] = ("PE", pe_h)
                    ac(lambda e, hp=hp, j=j: e.activation(
                        out=t8s[:, (j % 2) * BLK:(j % 2) * BLK + BLK],
                        in_=hp[:, 0:BLK], func=AF.Relu,
                        bias=b1c[:, j:j + 1], scale=1.0),
                       waits=[("PE", pe_h), h_prev], inc="ACT")
                    a_h = S.val("ACT")
                    pA_free[j % 2] = ("ACT", a_h)
                    ve(lambda e, j=j: e.tensor_copy(
                        out=hq_i8[:, j * BLK:(j + 1) * BLK],
                        in_=t8s.bitcast(I8)[:, (j % 2) * BLK:(j % 2) * BLK + BLK]),
                       waits=[("ACT", a_h)])
                    ve(lambda e: e.drain(), inc="VEC")
                    h_prev = ("VEC", S.val("VEC"))
                hq_ready = ("VEC", S.val("VEC"))
                if PH < 5:
                    for tt in range(NT_BLK):
                        store(dr_out[row0 + tt * 128:row0 + (tt + 1) * 128, :],
                              qT_sb[:, tt * D:(tt + 1) * D], waits=[attnO_v, hq_ready])
                    xnT_free = ("PE", S.val("PE"))
                    astk_free = ("PE", S.val("PE"))
                    qT_free = ("ST", n_stores[0] * 16)
                    pool_free = ("VEC", S.val("VEC"))
                    continue
                for tt in range(NT_BLK):
                    slot = tt % 2
                    pf = pB[slot]
                    for j in range(NJC):
                        for hf in range(NHD):
                            pe(lambda e, j=j, pf=pf, tt=tt, hf=hf: e.matmul(
                                pf[:, hf * 512:(hf + 1) * 512],
                                hq_f8[:, j * BLK + tt * 128:j * BLK + (tt + 1) * 128],
                                wq_f8[:, j * D + hf * 512:j * D + (hf + 1) * 512],
                                start=(j == 0), stop=(j == NJC - 1)),
                               waits=([hq_ready, pB_free[slot]]
                                      if (j == 0 and hf == 0) else []),
                               inc=("PE" if (j == NJC - 1 and hf == NHD - 1)
                                    else None))
                    pe_f = S.val("PE")
                    ve(lambda e, pf=pf: e.tensor_tensor(out=pool[0][:, 0:D],
                                                        in0=pf[:, 0:D],
                                                        in1=bq2_tile, op=ALU.add),
                       waits=[("PE", pe_f), ("ACT", a_bq2)])
                    ve(lambda e: e.drain(), inc="VEC")
                    pB_free[slot] = ("VEC", S.val("VEC"))
                    if PH >= 6:
                        ve(lambda e: e.tensor_copy(out=t16[:, 0:D],
                                                   in_=pool[0][:, 0:D]))
                        ve(lambda e: e.drain())
                        ve(lambda e: e.tensor_copy(out=i16s[:, 0:D],
                                                   in_=t16.bitcast(I16)[:, 0:D]))
                        ve(lambda e: e.drain())
                        ve(lambda e: e.tensor_copy(
                            out=pool[0][:, 0:D],
                            in_=i16s.bitcast(FP16)[:, 0:D]))
                        ve(lambda e: e.drain(), inc="VEC")
                    attnO = qT_sb
                    ve(lambda e, tt=tt: e.tensor_tensor(
                        out=attnO[:, tt * D:(tt + 1) * D],
                        in0=attnO[:, tt * D:(tt + 1) * D], in1=pool[0][:, 0:D],
                        op=ALU.add), waits=[attnO_v])
                    ve(lambda e: e.drain(), inc="VEC")
                    _v = S.val("VEC")
                    store(dr_out[row0 + tt * 128:row0 + (tt + 1) * 128, :],
                          attnO[:, tt * D:(tt + 1) * D], waits=[("VEC", _v)])
                xnT_free = ("PE", S.val("PE"))
                astk_free = ("PE", S.val("PE"))
                qT_free = ("ST", n_stores[0] * 16)
                pool_free = ("VEC", S.val("VEC"))
                xt_free = ("VEC", S.val("VEC"))

        S.emit("gpsimd", lambda e: e.engine_nop(), waits=[("ST", n_stores[0] * 16)])

        blkc = blk_ctx

        @blkc.sync
        def _(e):
            S.play("sync", e, sems)

        @blkc.tensor
        def _(e):
            S.play("tensor", e, sems)

        @blkc.vector
        def _(e):
            S.play("vector", e, sems)

        @blkc.scalar
        def _(e):
            S.play("scalar", e, sems)

        @blkc.gpsimd
        def _(e):
            S.play("gpsimd", e, sems)

    return nc


def _pack_inputs(cfg, xc, Wq, bq, Wk, bk, Wv, bv, Wo, bo, w1, b1, w2, b2):
    D = cfg["D"]
    NJC = cfg["HF"] // 128
    NDC = D // 128
    wqkv = np.ascontiguousarray(np.concatenate([Wq, Wk, Wv], axis=1), dtype=np.float32)
    bqkv = np.ascontiguousarray(np.concatenate([bq, bk, bv])[None, :], dtype=np.float32)
    wo_t = np.ascontiguousarray(Wo.reshape(NDC, 128, D), dtype=np.float32)
    w1t = np.ascontiguousarray(w1.T.reshape(D, NJC, 128).transpose(1, 0, 2),
                               dtype=np.float32)
    b1c = np.ascontiguousarray(b1.reshape(NJC, 128).T, dtype=np.float32)
    w2t = np.ascontiguousarray(w2.T, dtype=np.float32)
    return {
        "x": np.ascontiguousarray(xc.reshape(-1, D), dtype=np.float32),
        "wqkv": wqkv, "bqkv": bqkv,
        "wo": wo_t, "bo": np.ascontiguousarray(bo[None, :], dtype=np.float32),
        "w1t": w1t, "b1c": b1c,
        "w2t": w2t, "b2": np.ascontiguousarray(b2[None, :], dtype=np.float32),
    }


def kernel(L0_attn_x, L0_attn_Wq, L0_attn_bq, L0_attn_Wk, L0_attn_bk,
           L0_attn_Wv, L0_attn_bv, L0_attn_Wo, L0_attn_bo,
           L0_ffn_w1, L0_ffn_b1, L0_ffn_w2, L0_ffn_b2):
    x = np.asarray(L0_attn_x, dtype=np.float32)
    NB, T, D = x.shape
    n_cores = 8
    bpc = NB // n_cores
    cfg = dict(B=bpc, T=T, D=D, NH=16, NG=4, HF=4096, BLK=512)
    nc = build_program(cfg)
    args = [np.asarray(a, dtype=np.float32) for a in
            (L0_attn_Wq, L0_attn_bq, L0_attn_Wk, L0_attn_bk, L0_attn_Wv,
             L0_attn_bv, L0_attn_Wo, L0_attn_bo, L0_ffn_w1, L0_ffn_b1,
             L0_ffn_w2, L0_ffn_b2)]
    in_maps = [
        _pack_inputs(cfg, x[c * bpc:(c + 1) * bpc], *args) for c in range(n_cores)
    ]
    res = run_bass_kernel_spmd(nc, in_maps, core_ids=list(range(n_cores)))
    out = np.concatenate([r["out"].reshape(bpc, T, D) for r in res.results], axis=0)
    return out
